# version 06
# baseline (speedup 1.0000x reference)
import numpy as np
import ml_dtypes
from contextlib import ExitStack

import concourse.bass as bass
import concourse.mybir as mybir
from concourse.bass_utils import run_bass_kernel_spmd

F32 = mybir.dt.float32
BF16 = mybir.dt.bfloat16
AF = mybir.ActivationFunctionType
ALU = mybir.AluOpType
AX = mybir.AxisListType

C = 4096
D = 1024
CT = C // 128
G = 16
EPS = 1e-5
ISQ = 1.0 / 32.0


class Sem:
    def __init__(self, ctx, nc, name):
        self.h = ctx.enter_context(nc.semaphore(name))
        self.n = 0

    def inc(self, instr, k=1):
        instr.then_inc(self.h, k)
        self.n += k

    def rt(self, eng, instr):
        instr.then_inc(self.h, 1)
        self.n += 1
        eng.wait_ge(self.h, self.n)


def build_program(S=2048):
    T = S // 128
    QTS = 16 if T % 16 == 0 else T
    NQ = T // QTS
    SC = (QTS * 128) // 512
    FTS = 4
    assert T % FTS == 0
    NF = T // FTS
    AB = (T + 7) // 8

    nc = bass.Bass("TRN2", target_bir_lowering=False)

    x_h = nc.dram_tensor("x", [S, C], F32, kind="ExternalInput").ap()
    wq_h = nc.dram_tensor("wq", [8, 128, C], F32, kind="ExternalInput").ap()
    wk_h = nc.dram_tensor("wk", [8, 128, C], F32, kind="ExternalInput").ap()
    wv_h = nc.dram_tensor("wv", [8, 128, C], F32, kind="ExternalInput").ap()
    bq_h = nc.dram_tensor("bq", [128, 8], F32, kind="ExternalInput").ap()
    bk_h = nc.dram_tensor("bk", [128, 8], F32, kind="ExternalInput").ap()
    bv_h = nc.dram_tensor("bv", [128, 8], F32, kind="ExternalInput").ap()
    wo_h = nc.dram_tensor("wo", [8, 128, C], F32, kind="ExternalInput").ap()
    bo_h = nc.dram_tensor("bo", [128, C], F32, kind="ExternalInput").ap()
    up_h = nc.dram_tensor("upw", [G, 128, 2048], F32, kind="ExternalInput").ap()
    dn_h = nc.dram_tensor("dnw", [G, 128, 2048], F32, kind="ExternalInput").ap()
    id_h = nc.dram_tensor("ident", [128, 128], BF16, kind="ExternalInput").ap()

    out_h = nc.dram_tensor("out", [S, C], F32, kind="ExternalOutput").ap()

    q_hh = nc.dram_tensor("q_scr", [8, 128, S], BF16, kind="Internal").ap()
    k_hh = nc.dram_tensor("k_scr", [8, 128, S], BF16, kind="Internal").ap()
    v_hh = nc.dram_tensor("v_scr", [8, 128, S], BF16, kind="Internal").ap()
    x1_hh = nc.dram_tensor("x1_scr", [T, 128, C], BF16, kind="Internal").ap()
    wo_bf = nc.dram_tensor("wo_bf", [8, 128, C], BF16, kind="Internal").ap()
    up_bf = nc.dram_tensor("up_bf", [G, 128, 2048], BF16, kind="Internal").ap()
    dn_bf = nc.dram_tensor("dn_bf", [G, 128, 2048], BF16, kind="Internal").ap()

    with ExitStack() as gctx:
        identG = gctx.enter_context(nc.sbuf_tensor("identG", [128, 128], BF16))

        s_id = Sem(gctx, nc, "s_id")
        sx = [Sem(gctx, nc, f"q_sx{i}") for i in range(3)]
        sbl = Sem(gctx, nc, "q_sbl")
        ssq = Sem(gctx, nc, "q_ssq")
        sv2 = Sem(gctx, nc, "q_sv2")
        srs = Sem(gctx, nc, "q_srs")
        snm = Sem(gctx, nc, "q_snm")
        sh1 = Sem(gctx, nc, "q_sh1")
        stp = Sem(gctx, nc, "q_stp")
        sev = Sem(gctx, nc, "q_sev")
        swl = [Sem(gctx, nc, f"q_swl{i}") for i in range(2)]
        smm = Sem(gctx, nc, "q_smm")
        sqe = Sem(gctx, nc, "q_sqe")
        sqs = [Sem(gctx, nc, f"q_sqs{i}") for i in range(4)]
        skt = Sem(gctx, nc, "a_skt")
        svt = Sem(gctx, nc, "a_svt")
        swo = Sem(gctx, nc, "a_swo")
        sq = [Sem(gctx, nc, f"a_sq{i}") for i in range(2)]
        sxr = [Sem(gctx, nc, f"a_sxr{i}") for i in range(2)]
        s_sc = Sem(gctx, nc, "a_ssc")
        s_mx = Sem(gctx, nc, "a_smx")
        s_ex = Sem(gctx, nc, "a_sex")
        s_at = Sem(gctx, nc, "a_sat")
        s_ae = Sem(gctx, nc, "a_sae")
        s_oa = Sem(gctx, nc, "a_soa")
        s_oe = Sem(gctx, nc, "a_soe")
        s_ot = Sem(gctx, nc, "a_sot")
        s_ote = Sem(gctx, nc, "a_sote")
        s_pj = Sem(gctx, nc, "a_spj")
        s_pje = Sem(gctx, nc, "a_spje")
        s_q2 = Sem(gctx, nc, "a_sq2")
        s_v2 = Sem(gctx, nc, "a_sv2")
        s_r2 = Sem(gctx, nc, "a_sr2")
        s_n2 = Sem(gctx, nc, "a_sn2")
        s_nm = Sem(gctx, nc, "a_snm")
        s_x1 = Sem(gctx, nc, "a_sx1")
        s_x1s = [Sem(gctx, nc, f"a_sx1s{i}") for i in range(2)]
        s_q3 = Sem(gctx, nc, "a_sq3")
        s_wc = Sem(gctx, nc, "q_swc")
        s_wol = Sem(gctx, nc, "a_swol")
        s_acc = [Sem(gctx, nc, f"f_sacc{i}") for i in range(2)]
        s_v3 = Sem(gctx, nc, "a_sv3")
        s_r3 = Sem(gctx, nc, "a_sr3")
        sxl = Sem(gctx, nc, "f_sxl")
        ssq3 = Sem(gctx, nc, "f_ssq3")
        sv3 = Sem(gctx, nc, "f_sv3")
        sr3 = Sem(gctx, nc, "f_sr3")
        sn3 = Sem(gctx, nc, "f_sn3")
        sh2 = Sem(gctx, nc, "f_sh2")
        stp3 = Sem(gctx, nc, "f_stp3")
        sev3 = Sem(gctx, nc, "f_sev3")
        swf = [Sem(gctx, nc, f"f_swf{i}") for i in range(4)]
        sup = Sem(gctx, nc, "f_sup")
        sgel = Sem(gctx, nc, "f_sgel")
        sdnt = Sem(gctx, nc, "f_sdnt")
        sfev = Sem(gctx, nc, "f_sfev")
        ssq4 = Sem(gctx, nc, "f_ssq4")
        sv4 = Sem(gctx, nc, "f_sv4")
        sr4 = Sem(gctx, nc, "f_sr4")
        sn4 = Sem(gctx, nc, "f_sn4")
        snm4 = Sem(gctx, nc, "f_snm4")
        sadd = Sem(gctx, nc, "f_sadd")
        sost = [Sem(gctx, nc, f"f_sost{i}") for i in range(2)]
        s1_3 = gctx.enter_context(nc.sbuf_tensor("s1_3", [128, T], F32))
        s2_3 = gctx.enter_context(nc.sbuf_tensor("s2_3", [128, T], F32))
        mean3 = gctx.enter_context(nc.sbuf_tensor("mean3", [128, T], F32))
        var3 = gctx.enter_context(nc.sbuf_tensor("var3", [128, T], F32))
        msq3 = gctx.enter_context(nc.sbuf_tensor("msq3", [128, T], F32))
        lnv3 = gctx.enter_context(nc.sbuf_tensor("lnv3", [128, T], F32))
        rstd3 = gctx.enter_context(nc.sbuf_tensor("rstd3", [128, T], F32))
        nmr3 = gctx.enter_context(nc.sbuf_tensor("nmr3", [128, T], F32))
        qch_v = Sem(gctx, nc, "qch_v")
        qch_s = Sem(gctx, nc, "qch_s")
        ach_v = Sem(gctx, nc, "ach_v")
        ach_s = Sem(gctx, nc, "ach_s")
        fch_v = Sem(gctx, nc, "fch_v")
        fch_s = Sem(gctx, nc, "fch_s")

        with ExitStack() as ctx:
            E = ctx.enter_context
            xb = [E(nc.sbuf_tensor(f"xb{i}", [128, C], BF16)) for i in range(3)]
            h1 = [E(nc.sbuf_tensor(f"h1_{i}", [128, C], BF16)) for i in range(2)]
            h1T = E(nc.sbuf_tensor("h1T", [128, CT, QTS * 128], BF16))
            wbuf = [E(nc.sbuf_tensor(f"wbuf{i}", [128, CT, 128], BF16)) for i in range(2)]
            vscr = E(nc.sbuf_tensor("vscr", [128, C], BF16))
            qev = [E(nc.sbuf_tensor(f"qev{i}", [128, 512], BF16)) for i in range(4)]
            bq_s = E(nc.sbuf_tensor("bq_s", [128, 8], F32))
            bk_s = E(nc.sbuf_tensor("bk_s", [128, 8], F32))
            bv_s = E(nc.sbuf_tensor("bv_s", [128, 8], F32))
            s1q = E(nc.sbuf_tensor("s1q", [128, T], F32))
            s2q = E(nc.sbuf_tensor("s2q", [128, T], F32))
            meanq = E(nc.sbuf_tensor("meanq", [128, T], F32))
            varq = E(nc.sbuf_tensor("varq", [128, T], F32))
            msqq = E(nc.sbuf_tensor("msqq", [128, T], F32))
            lnvq = E(nc.sbuf_tensor("lnvq", [128, T], F32))
            rstdq = E(nc.sbuf_tensor("rstdq", [128, T], F32))
            nmrq = E(nc.sbuf_tensor("nmrq", [128, T], F32))
            ps_t = [E(nc.psum_tensor(f"ps_t{i}", [128, 8, 128], BF16)) for i in range(2)]
            ps_q = [E(nc.psum_tensor(f"ps_q{i}", [128, 512], F32)) for i in range(4)]

            with nc.Block() as blk:
                @blk.gpsimd
                def _(g):
                    g.dma_start(out=identG[:], in_=id_h[:]).then_inc(s_id.h, 16)
                    s_id.n += 16
                    for name, dst, src in (("bq", bq_s, bq_h), ("bk", bk_s, bk_h),
                                           ("bv", bv_s, bv_h)):
                        d = g.dma_start(out=dst[:], in_=src[:])
                        sbl.inc(d, 16)
                    for ch in range(NQ):
                        for i in range(QTS):
                            t = ch * QTS + i
                            if t >= 3:
                                g.wait_ge(sh1.h, t - 2)
                                g.wait_ge(sx[t % 3].h, 16 * (t // 3))
                            d = g.dma_start(out=xb[t % 3][:],
                                            in_=x_h[t * 128:(t + 1) * 128, :])
                            sx[t % 3].inc(d, 16)
                        for f in range(24):
                            fg = ch * 24 + f
                            if fg >= 2:
                                g.wait_ge(smm.h, SC * (fg - 1))
                                g.wait_ge(swl[fg % 2].h, 16 * (fg // 2))
                            src = wq_h[f] if f < 8 else (
                                wk_h[f - 8] if f < 16 else wv_h[f - 16])
                            d = g.dma_start(out=wbuf[fg % 2][:], in_=src)
                            swl[fg % 2].inc(d, 16)
                    for dt in range(8):
                        d = g.dma_start(out=wo_bf[dt], in_=wo_h[dt])
                        s_wc.inc(d, 16)
                    for gi in range(G):
                        d = g.dma_start(out=up_bf[gi], in_=up_h[gi])
                        s_wc.inc(d, 16)
                        d = g.dma_start(out=dn_bf[gi], in_=dn_h[gi])
                        s_wc.inc(d, 16)
                    g.wait_ge(s_wc.h, s_wc.n)

                @blk.sync
                def _(sy):
                    for ch in range(NQ):
                        for f in range(24):
                            for s_i in range(SC):
                                e = (ch * 24 + f) * SC + s_i
                                sy.wait_ge(sqe.h, e + 1)
                                if e >= 4:
                                    sy.wait_ge(sqs[e % 4].h, 16 * (e // 4))
                                off = ch * QTS * 128 + s_i * 512
                                if f < 8:
                                    dst = q_hh[f, :, off:off + 512]
                                elif f < 16:
                                    dst = k_hh[f - 8, :, off:off + 512]
                                else:
                                    dst = v_hh[f - 16, :, off:off + 512]
                                d = sy.dma_start(out=dst, in_=qev[e % 4][:])
                                sqs[e % 4].inc(d, 16)
                    for ss in sqs:
                        sy.wait_ge(ss.h, ss.n)

                @blk.scalar
                def _(s):
                    s.wait_ge(sbl.h, 48)
                    for ch in range(NQ):
                        for i in range(QTS):
                            t = ch * QTS + i
                            s.wait_ge(sv2.h, t + 1)
                            a = s.activation(lnvq[:, t:t + 1], varq[:, t:t + 1], AF.Ln)
                            qch_s.rt(s, a)
                            a = s.activation(rstdq[:, t:t + 1], lnvq[:, t:t + 1],
                                             AF.Exp, scale=-0.5)
                            srs.inc(a)
                            s.wait_ge(snm.h, t + 1)
                            if t >= 2:
                                s.wait_ge(stp.h, 4 * (t - 1))
                            a = s.activation(h1[t % 2][:], xb[t % 3][:], AF.Identity,
                                             bias=nmrq[:, t:t + 1],
                                             scale=rstdq[:, t:t + 1])
                            sh1.inc(a)
                        for f in range(24):
                            for s_i in range(SC):
                                grp = (ch * 24 + f) * SC + s_i
                                s.wait_ge(smm.h, grp + 1)
                                if grp >= 4:
                                    s.wait_ge(sqs[grp % 4].h, 16 * (grp // 4))
                                if f < 8:
                                    bias = bq_s[:, f:f + 1]
                                elif f < 16:
                                    bias = bk_s[:, f - 8:f - 7]
                                else:
                                    bias = bv_s[:, f - 16:f - 15]
                                a = s.activation(qev[grp % 4][:], ps_q[grp % 4][:],
                                                 AF.Identity, bias=bias)
                                sqe.inc(a)

                @blk.vector
                def _(v):
                    for t in range(T):
                        v.wait_ge(sx[t % 3].h, 16 * (t // 3 + 1))
                        a = v.tensor_scalar(out=vscr[:], in0=xb[t % 3][:],
                                            scalar1=1.0, scalar2=None,
                                            op0=ALU.mult, op1=ALU.add,
                                            accum_out=s1q[:, t:t + 1])
                        qch_v.rt(v, a)
                        a = v.scalar_tensor_tensor(out=vscr[:], in0=xb[t % 3][:],
                                                   scalar=1.0, in1=xb[t % 3][:],
                                                   op0=ALU.mult, op1=ALU.mult,
                                                   accum_out=s2q[:, t:t + 1])
                        qch_v.rt(v, a)
                        a = v.tensor_scalar(out=meanq[:, t:t + 1], in0=s1q[:, t:t + 1],
                                            scalar1=1.0 / C, scalar2=None, op0=ALU.mult)
                        qch_v.rt(v, a)
                        v.tensor_mul(msqq[:, t:t + 1], meanq[:, t:t + 1],
                                     meanq[:, t:t + 1])
                        a = v.tensor_scalar(out=varq[:, t:t + 1], in0=s2q[:, t:t + 1],
                                            scalar1=1.0 / C, scalar2=EPS,
                                            op0=ALU.mult, op1=ALU.add)
                        qch_v.rt(v, a)
                        a = v.tensor_sub(varq[:, t:t + 1], varq[:, t:t + 1],
                                         msqq[:, t:t + 1])
                        sv2.inc(a)
                        v.wait_ge(srs.h, t + 1)
                        a = v.scalar_tensor_tensor(out=nmrq[:, t:t + 1],
                                                   in0=meanq[:, t:t + 1],
                                                   scalar=-1.0, in1=rstdq[:, t:t + 1],
                                                   op0=ALU.mult, op1=ALU.mult)
                        snm.inc(a)
                        i = t % QTS
                        for b in range(4):
                            bg = t * 4 + b
                            v.wait_ge(stp.h, bg + 1)
                            a = v.tensor_copy(
                                h1T[:, 8 * b:8 * b + 8, i * 128:(i + 1) * 128],
                                ps_t[bg % 2][:])
                            sev.inc(a)

                @blk.tensor
                def _(tn):
                    tn.wait_ge(s_id.h, 16)
                    for ch in range(NQ):
                        for i in range(QTS):
                            t = ch * QTS + i
                            tn.wait_ge(sh1.h, t + 1)
                            for b in range(4):
                                bg = t * 4 + b
                                if bg >= 2:
                                    tn.wait_ge(sev.h, bg - 1)
                                last = None
                                for j in range(8):
                                    ct = 8 * b + j
                                    last = tn.transpose(
                                        ps_t[bg % 2][:, j, :],
                                        h1[t % 2][:, ct * 128:(ct + 1) * 128],
                                        identG[:])
                                stp.inc(last)
                        for f in range(24):
                            fg = ch * 24 + f
                            tn.wait_ge(swl[fg % 2].h, 16 * (fg // 2 + 1))
                            for s_i in range(SC):
                                grp = fg * SC + s_i
                                need = ch * QTS + min(4 * (s_i + 1), QTS)
                                tn.wait_ge(sev.h, 4 * need)
                                if grp >= 4:
                                    tn.wait_ge(sqe.h, grp - 3)
                                last = None
                                for ct in range(CT):
                                    last = tn.matmul(
                                        ps_q[grp % 4][:],
                                        wbuf[fg % 2][:, ct, :],
                                        h1T[:, ct, s_i * 512:(s_i + 1) * 512],
                                        start=(ct == 0), stop=(ct == CT - 1))
                                smm.inc(last)

        with ExitStack() as ctx:
            E = ctx.enter_context
            kT = E(nc.sbuf_tensor("kT", [128, 8, S], BF16))
            v_sb = E(nc.sbuf_tensor("v_sb", [128, T, D], BF16))
            wo_sb = E(nc.sbuf_tensor("wo_sb", [128, 8, C], BF16))
            bo_sb = E(nc.sbuf_tensor("bo_sb", [128, C], BF16))
            qt = [E(nc.sbuf_tensor(f"qt{i}", [128, 8, 128], BF16)) for i in range(2)]
            probs = E(nc.sbuf_tensor("probs", [128, S], BF16))
            attnT = E(nc.sbuf_tensor("attnT", [128, T, 128], BF16))
            o_sb = E(nc.sbuf_tensor("o_sb", [128, D], BF16))
            oT = [E(nc.sbuf_tensor(f"oT{i}", [128, 8, 128], BF16)) for i in range(2)]
            pb2 = [E(nc.sbuf_tensor(f"pb2_{i}", [128, C], BF16)) for i in range(2)]
            xr = [E(nc.sbuf_tensor(f"xr{i}", [128, C], BF16)) for i in range(2)]
            x1b = [E(nc.sbuf_tensor(f"x1b{i}", [128, C], BF16)) for i in range(2)]
            negmax = E(nc.sbuf_tensor("negmax", [128, T], F32))
            nmax32 = E(nc.sbuf_tensor("nmax32", [128, T], F32))
            sumexp = E(nc.sbuf_tensor("sumexp", [128, T], F32))
            rs_t = E(nc.sbuf_tensor("rs_t", [128, T], F32))
            s1p = E(nc.sbuf_tensor("s1p", [128, 8], F32))
            s1a = E(nc.sbuf_tensor("s1a", [128, T], F32))
            s2a = E(nc.sbuf_tensor("s2a", [128, T], F32))
            meana = E(nc.sbuf_tensor("meana", [128, T], F32))
            vara = E(nc.sbuf_tensor("vara", [128, T], F32))
            msqa = E(nc.sbuf_tensor("msqa", [128, T], F32))
            lnva = E(nc.sbuf_tensor("lnva", [128, T], F32))
            rstda = E(nc.sbuf_tensor("rstda", [128, T], F32))
            nmra = E(nc.sbuf_tensor("nmra", [128, T], F32))
            ps_s = E(nc.psum_tensor("ps_s", [128, S], F32))
            ps_at = E(nc.psum_tensor("ps_at", [128, 8, 128], BF16))
            ps_p = [E(nc.psum_tensor(f"ps_p{i}", [128, 512], F32))
                    for i in range(3)]
            if S >= 2 * D:
                ps_o = ps_s[:, S - D:S]
            else:
                ps_o = E(nc.psum_tensor("ps_o", [128, D], F32))

            with nc.Block() as blk:
                @blk.sync
                def _(sy):
                    for dt in range(8):
                        d = sy.dma_start(out=kT[:, dt, :], in_=k_hh[dt])
                        skt.inc(d, 16)
                    for dt in range(8):
                        d = sy.dma_start(out=wo_sb[:, dt, :], in_=wo_bf[dt])
                        s_wol.inc(d, 16)
                    for kt_i in range(T):
                        src = v_hh[:, :, kt_i * 128:(kt_i + 1) * 128].rearrange(
                            "a p m -> (a p) m")
                        d = sy.dma_start_transpose(v_sb[:, kt_i, :], src)
                        svt.inc(d, 16)
                    for t in range(T):
                        if t >= 2:
                            sy.wait_ge(s_sc.h, t - 1)
                            sy.wait_ge(sq[t % 2].h, 16 * (t // 2))
                        src = q_hh[:, :, t * 128:(t + 1) * 128].rearrange(
                            "a p m -> p a m")
                        d = sy.dma_start(out=qt[t % 2][:], in_=src)
                        sq[t % 2].inc(d, 16)

                @blk.gpsimd
                def _(g):
                    d = g.dma_start(out=bo_sb[:], in_=bo_h[:])
                    swo.inc(d, 16)
                    for t in range(T):
                        if t >= 2:
                            g.wait_ge(s_x1.h, t - 1)
                            g.wait_ge(sxr[t % 2].h, 16 * (t // 2))
                        d = g.dma_start(out=xr[t % 2][:],
                                        in_=x_h[t * 128:(t + 1) * 128, :])
                        sxr[t % 2].inc(d, 16)
                        if t >= 1:
                            j = t - 1
                            g.wait_ge(s_x1.h, t)
                            if j >= 2:
                                g.wait_ge(s_x1s[j % 2].h, 16 * (j // 2))
                            d = g.dma_start(out=x1_hh[j], in_=x1b[j % 2][:])
                            s_x1s[j % 2].inc(d, 16)
                    j = T - 1
                    g.wait_ge(s_x1.h, T)
                    g.wait_ge(s_x1s[j % 2].h, 16 * (j // 2))
                    d = g.dma_start(out=x1_hh[j], in_=x1b[j % 2][:])
                    s_x1s[j % 2].inc(d, 16)
                    for ss in s_x1s:
                        g.wait_ge(ss.h, ss.n)

                @blk.tensor
                def _(tn):
                    for t in range(T + 1):
                        if t >= 1:
                            tn.wait_ge(s_oe.h, t)
                            tn.wait_ge(s_ae.h, AB * t)
                            if t >= 2:
                                tn.wait_ge(s_ote.h, t - 1)
                            last = None
                            for dt in range(8):
                                last = tn.transpose(
                                    ps_at[:, dt, :],
                                    o_sb[:, dt * 128:(dt + 1) * 128], identG[:])
                            s_ot.inc(last)
                        if t < T:
                            tn.wait_ge(sq[t % 2].h, 16 * (t // 2 + 1))
                            if t == 0:
                                tn.wait_ge(skt.h, skt.n)
                            if t >= 1:
                                tn.wait_ge(s_ex.h, t)
                                if S >= 2 * D:
                                    tn.wait_ge(s_oe.h, t)
                            last = None
                            for s_i in range(S // 512):
                                for dt in range(8):
                                    last = tn.matmul(
                                        ps_s[:, s_i * 512:(s_i + 1) * 512],
                                        qt[t % 2][:, dt, :],
                                        kT[:, dt, s_i * 512:(s_i + 1) * 512],
                                        start=(dt == 0), stop=(dt == 7))
                            s_sc.inc(last)
                        if t >= 1:
                            tn.wait_ge(s_ote.h, t)
                            if t == 1:
                                tn.wait_ge(swo.h, swo.n)
                                tn.wait_ge(s_wol.h, s_wol.n)
                            for cq in range(8):
                                pg = 8 * (t - 1) + cq
                                if pg >= 3:
                                    tn.wait_ge(s_pje.h, pg - 2)
                                last = None
                                for dt in range(8):
                                    last = tn.matmul(
                                        ps_p[pg % 3][:],
                                        oT[(t - 1) % 2][:, dt, :],
                                        wo_sb[:, dt, cq * 512:(cq + 1) * 512],
                                        start=(dt == 0), stop=(dt == 7))
                                s_pj.inc(last)
                        if t < T:
                            tn.wait_ge(s_ex.h, t + 1)
                            for b in range(AB):
                                bg = AB * t + b
                                if bg >= 1:
                                    tn.wait_ge(s_ae.h, bg)
                                nb = min(8, T - 8 * b)
                                last = None
                                for j in range(nb):
                                    kt_i = 8 * b + j
                                    last = tn.transpose(
                                        ps_at[:, j, :],
                                        probs[:, kt_i * 128:(kt_i + 1) * 128],
                                        identG[:])
                                s_at.inc(last)
                            if t == 0:
                                tn.wait_ge(svt.h, svt.n)
                            if t >= 1:
                                tn.wait_ge(s_oe.h, t)
                            last1 = None
                            for kt_i in range(T):
                                if kt_i % 8 == 0:
                                    tn.wait_ge(s_ae.h, AB * t + kt_i // 8 + 1)
                                src = attnT[:, kt_i, :]
                                tn.matmul(ps_o[:, 0:512], src,
                                          v_sb[:, kt_i, 0:512],
                                          start=(kt_i == 0), stop=(kt_i == T - 1),
                                          skip_group_check=True)
                                last1 = tn.matmul(ps_o[:, 512:1024], src,
                                                  v_sb[:, kt_i, 512:1024],
                                                  start=(kt_i == 0),
                                                  stop=(kt_i == T - 1),
                                                  skip_group_check=True)
                            s_oa.inc(last1)

                @blk.vector
                def _(v):
                    for t in range(T + 1):
                        if t >= 1:
                            v.wait_ge(s_ot.h, t)
                            if t >= 3:
                                v.wait_ge(s_pj.h, 8 * (t - 2))
                            a = v.tensor_copy(oT[(t - 1) % 2][:], ps_at[:])
                            s_ote.inc(a)
                        if t < T:
                            v.wait_ge(s_sc.h, t + 1)
                            a = v.reduce_max(negmax[:, t:t + 1], ps_s[:], axis=AX.X,
                                             negate=True)
                            ach_v.rt(v, a)
                            a = v.tensor_scalar(out=nmax32[:, t:t + 1],
                                                in0=negmax[:, t:t + 1],
                                                scalar1=ISQ, scalar2=None,
                                                op0=ALU.mult)
                            s_mx.inc(a)
                        if t >= 1:
                            for cq in range(8):
                                pg = 8 * (t - 1) + cq
                                v.wait_ge(s_pj.h, pg + 1)
                                if cq == 0 and t >= 3:
                                    v.wait_ge(s_nm.h, t - 2)
                                a = v.scalar_tensor_tensor(
                                    out=pb2[(t - 1) % 2][:, cq * 512:(cq + 1) * 512],
                                    in0=ps_p[pg % 3][:], scalar=rs_t[:, t - 1:t],
                                    in1=bo_sb[:, cq * 512:(cq + 1) * 512],
                                    op0=ALU.mult, op1=ALU.add,
                                    accum_out=s1p[:, cq:cq + 1])
                                s_pje.inc(a)
                        if t < T:
                            for b in range(AB):
                                v.wait_ge(s_at.h, AB * t + b + 1)
                                if t >= 1 and b == 0:
                                    v.wait_ge(s_oa.h, t)
                                nb = min(8, T - 8 * b)
                                a = v.tensor_copy(attnT[:, 8 * b:8 * b + nb, :],
                                                  ps_at[:, 0:nb, :])
                                s_ae.inc(a)
                            v.wait_ge(s_ex.h, t + 1)
                            a = v.reciprocal(rs_t[:, t:t + 1], sumexp[:, t:t + 1])
                            ach_v.rt(v, a)
                        if t >= 1:
                            v.wait_ge(s_pje.h, 8 * t)
                            a = v.reduce_sum(s1a[:, t - 1:t], s1p[:], axis=AX.X)
                            ach_v.rt(v, a)
                            a = v.tensor_scalar(out=meana[:, t - 1:t],
                                                in0=s1a[:, t - 1:t],
                                                scalar1=1.0 / C, scalar2=None,
                                                op0=ALU.mult)
                            ach_v.rt(v, a)
                            v.wait_ge(s_q2.h, t)
                            v.tensor_mul(msqa[:, t - 1:t], meana[:, t - 1:t],
                                         meana[:, t - 1:t])
                            a = v.tensor_scalar(out=vara[:, t - 1:t],
                                                in0=s2a[:, t - 1:t],
                                                scalar1=1.0 / C, scalar2=EPS,
                                                op0=ALU.mult, op1=ALU.add)
                            ach_v.rt(v, a)
                            a = v.tensor_sub(vara[:, t - 1:t], vara[:, t - 1:t],
                                             msqa[:, t - 1:t])
                            s_v2.inc(a)
                        if t >= 1:
                            v.wait_ge(s_r2.h, t)
                            a = v.scalar_tensor_tensor(
                                out=nmra[:, t - 1:t], in0=meana[:, t - 1:t],
                                scalar=-1.0, in1=rstda[:, t - 1:t],
                                op0=ALU.mult, op1=ALU.mult)
                            s_n2.inc(a)
                            v.wait_ge(s_nm.h, t)
                            v.wait_ge(sxr[(t - 1) % 2].h, 16 * ((t - 1) // 2 + 1))
                            a = v.tensor_add(x1b[(t - 1) % 2][:],
                                             x1b[(t - 1) % 2][:],
                                             xr[(t - 1) % 2][:])
                            s_x1.inc(a)
                            v.wait_ge(s_x1.h, t)
                            a = v.reduce_sum(s1_3[:, t - 1:t],
                                             x1b[(t - 1) % 2][:], axis=AX.X)
                            ach_v.rt(v, a)
                            a = v.tensor_scalar(out=mean3[:, t - 1:t],
                                                in0=s1_3[:, t - 1:t],
                                                scalar1=1.0 / C, scalar2=None,
                                                op0=ALU.mult)
                            ach_v.rt(v, a)
                            v.wait_ge(s_q3.h, t)
                            v.tensor_mul(msq3[:, t - 1:t], mean3[:, t - 1:t],
                                         mean3[:, t - 1:t])
                            a = v.tensor_scalar(out=var3[:, t - 1:t],
                                                in0=s2_3[:, t - 1:t],
                                                scalar1=1.0 / C, scalar2=EPS,
                                                op0=ALU.mult, op1=ALU.add)
                            ach_v.rt(v, a)
                            a = v.tensor_sub(var3[:, t - 1:t], var3[:, t - 1:t],
                                             msq3[:, t - 1:t])
                            s_v3.inc(a)
                            v.wait_ge(s_r3.h, t)
                            a = v.scalar_tensor_tensor(
                                out=nmr3[:, t - 1:t], in0=mean3[:, t - 1:t],
                                scalar=-1.0, in1=rstd3[:, t - 1:t],
                                op0=ALU.mult, op1=ALU.mult)
                            ach_v.rt(v, a)

                @blk.scalar
                def _(s):
                    for t in range(T + 1):
                        if t < T:
                            s.wait_ge(s_mx.h, t + 1)
                            if t >= 1:
                                s.wait_ge(s_at.h, AB * t)
                            a = s.activation(probs[:], ps_s[:], AF.Exp,
                                             bias=nmax32[:, t:t + 1], scale=ISQ,
                                             accum_out=sumexp[:, t:t + 1])
                            s_ex.inc(a)
                        if t < T:
                            s.wait_ge(s_oa.h, t + 1)
                            s.wait_ge(s_ot.h, t)
                            a = s.activation(o_sb[:], ps_o[:], AF.Identity)
                            s_oe.inc(a)
                        if t >= 1:
                            s.wait_ge(s_pje.h, 8 * t)
                            if t >= 3:
                                s.wait_ge(s_x1s[(t - 1) % 2].h,
                                          16 * ((t - 3) // 2 + 1))
                            a = s.activation(x1b[(t - 1) % 2][:],
                                             pb2[(t - 1) % 2][:],
                                             AF.Square, accum_out=s2a[:, t - 1:t])
                            s_q2.inc(a)
                        if t >= 1:
                            s.wait_ge(s_v2.h, t)
                            a = s.activation(lnva[:, t - 1:t], vara[:, t - 1:t], AF.Ln)
                            ach_s.rt(s, a)
                            a = s.activation(rstda[:, t - 1:t], lnva[:, t - 1:t],
                                             AF.Exp, scale=-0.5)
                            s_r2.inc(a)
                            s.wait_ge(s_n2.h, t)
                            a = s.activation(x1b[(t - 1) % 2][:],
                                             pb2[(t - 1) % 2][:], AF.Identity,
                                             bias=nmra[:, t - 1:t],
                                             scale=rstda[:, t - 1:t])
                            s_nm.inc(a)
                            s.wait_ge(s_x1.h, t)
                            a = s.activation(pb2[(t - 1) % 2][:],
                                             x1b[(t - 1) % 2][:], AF.Square,
                                             accum_out=s2_3[:, t - 1:t])
                            s_q3.inc(a)
                            s.wait_ge(s_v3.h, t)
                            a = s.activation(lnv3[:, t - 1:t], var3[:, t - 1:t],
                                             AF.Ln)
                            ach_s.rt(s, a)
                            a = s.activation(rstd3[:, t - 1:t], lnv3[:, t - 1:t],
                                             AF.Exp, scale=-0.5)
                            s_r3.inc(a)

        with ExitStack() as ctx:
            E = ctx.enter_context
            x1c = E(nc.sbuf_tensor("x1c", [128, FTS, C], BF16))
            h2 = [E(nc.sbuf_tensor(f"h2_{i}", [128, C], BF16)) for i in range(2)]
            h2T = E(nc.sbuf_tensor("h2T", [128, CT, FTS * 128], BF16))
            up_sb = [E(nc.sbuf_tensor(f"up_sb{i}", [128, 2, 8, 128], BF16))
                     for i in range(4)]
            dn_sb = [E(nc.sbuf_tensor(f"dn_sb{i}", [128, 8, 256], BF16))
                     for i in range(4)]
            midT = [E(nc.sbuf_tensor(f"midT{i}", [128, 8, FTS * 128], BF16))
                    for i in range(2)]
            f_raw = E(nc.sbuf_tensor("f_raw", [128, FTS, C], BF16))
            scr3 = E(nc.sbuf_tensor("scr3", [128, C], BF16))
            outb = [E(nc.sbuf_tensor(f"outb{i}", [128, C], F32)) for i in range(2)]
            s1p4 = E(nc.sbuf_tensor("s1p4", [128, FTS, G], F32))
            s1_4 = E(nc.sbuf_tensor("s1_4", [128, T], F32))
            s2_4 = E(nc.sbuf_tensor("s2_4", [128, T], F32))
            mean4 = E(nc.sbuf_tensor("mean4", [128, T], F32))
            var4 = E(nc.sbuf_tensor("var4", [128, T], F32))
            msq4 = E(nc.sbuf_tensor("msq4", [128, T], F32))
            lnv4 = E(nc.sbuf_tensor("lnv4", [128, T], F32))
            rstd4 = E(nc.sbuf_tensor("rstd4", [128, T], F32))
            nmr4 = E(nc.sbuf_tensor("nmr4", [128, T], F32))
            ps_m = [E(nc.psum_tensor(f"ps_m{i}", [128, 2, 512], F32))
                    for i in range(2)]
            ps_f = [E(nc.psum_tensor(f"ps_f{i}", [128, 512], F32)) for i in range(2)]
            ps_t3 = [E(nc.psum_tensor(f"ps_t3{i}", [128, 8, 128], BF16))
                     for i in range(2)]

            with nc.Block() as blk:
                @blk.sync
                def _(sy):
                    for c in range(NF):
                        if c >= 1:
                            sy.wait_ge(sh2.h, FTS * c)
                            sy.wait_ge(sxl.h, 16 * c)
                        src = x1_hh[c * FTS:(c + 1) * FTS].rearrange(
                            "a p m -> p a m")
                        d = sy.dma_start(out=x1c[:], in_=src)
                        sxl.inc(d, 16)
                        for gi in range(G):
                            gg = c * G + gi
                            if gg >= 4:
                                sy.wait_ge(sdnt.h, FTS * (gg - 3))
                                sy.wait_ge(swf[gg % 4].h, 32 * (gg // 4))
                            d = sy.dma_start(out=up_sb[gi % 4][:], in_=up_bf[gi])
                            swf[gg % 4].inc(d, 16)
                            d = sy.dma_start(out=dn_sb[gi % 4][:], in_=dn_bf[gi])
                            swf[gg % 4].inc(d, 16)
                        for tl in range(FTS):
                            tg = c * FTS + tl
                            sy.wait_ge(s_acc[tg % 2].h, 32 * (tg // 2 + 1))
                            if tg >= 2:
                                sy.wait_ge(sost[tg % 2].h, 16 * (tg // 2))
                            d = sy.dma_start(out=out_h[tg * 128:(tg + 1) * 128, :],
                                             in_=outb[tl % 2][:])
                            sost[tg % 2].inc(d, 16)
                    for ss in sost:
                        sy.wait_ge(ss.h, ss.n)

                @blk.gpsimd
                def _(g):
                    for c in range(NF):
                        for tl in range(FTS):
                            tg = c * FTS + tl
                            g.wait_ge(snm4.h, tg + 1)
                            if tg >= 2:
                                g.wait_ge(s_acc[tg % 2].h, 32 * (tg // 2))
                            d = g.dma_start(out=outb[tl % 2][:, 0:2048],
                                            in_=x1_hh[tg][:, 0:2048],
                                            accum_op=ALU.add)
                            s_acc[tg % 2].inc(d, 16)
                            d = g.dma_start(out=outb[tl % 2][:, 2048:C],
                                            in_=x1_hh[tg][:, 2048:C],
                                            accum_op=ALU.add)
                            s_acc[tg % 2].inc(d, 16)


                @blk.scalar
                def _(s):
                    for c in range(NF):
                        s.wait_ge(sxl.h, 16 * (c + 1))
                        for tl in range(FTS):
                            tg = c * FTS + tl
                            if tg >= 2:
                                s.wait_ge(stp3.h, 4 * (tg - 1))
                            a = s.activation(h2[tg % 2][:], x1c[:, tl, :], AF.Identity,
                                             bias=nmr3[:, tg:tg + 1],
                                             scale=rstd3[:, tg:tg + 1])
                            sh2.inc(a)
                        for gi in range(G):
                            gg = c * G + gi
                            for q in range(4):
                                s.wait_ge(sup.h, 4 * gg + q + 1)
                                if gg >= 2 and q == 0:
                                    s.wait_ge(sdnt.h, FTS * (gg - 1))
                                a = s.activation(midT[gi % 2][:, 2 * q:2 * q + 2, :],
                                                 ps_m[q % 2][:], AF.Gelu)
                                sgel.inc(a)
                        s.wait_ge(sfev.h, FTS * G * (c + 1))
                        for tl in range(FTS):
                            tg = c * FTS + tl
                            if tg >= 1:
                                s.wait_ge(ssq4.h, tg)
                            a = s.activation(scr3[:], f_raw[:, tl, :],
                                             AF.Square,
                                             accum_out=s2_4[:, tg:tg + 1])
                            ssq4.inc(a)
                        s.wait_ge(sv4.h, FTS * (c + 1))
                        a = s.activation(lnv4[:, c * FTS:(c + 1) * FTS],
                                         var4[:, c * FTS:(c + 1) * FTS], AF.Ln)
                        fch_s.rt(s, a)
                        a = s.activation(rstd4[:, c * FTS:(c + 1) * FTS],
                                         lnv4[:, c * FTS:(c + 1) * FTS],
                                         AF.Exp, scale=-0.5)
                        sr4.inc(a)
                        s.wait_ge(sn4.h, c + 1)
                        for tl in range(FTS):
                            tg = c * FTS + tl
                            if tg >= 2:
                                s.wait_ge(sost[tg % 2].h, 16 * (tg // 2))
                            a = s.activation(outb[tl % 2][:], f_raw[:, tl, :],
                                             AF.Identity, bias=nmr4[:, tg:tg + 1],
                                             scale=rstd4[:, tg:tg + 1])
                            snm4.inc(a)

                @blk.vector
                def _(v):
                    for c in range(NF):
                        for tl in range(FTS):
                            tg = c * FTS + tl
                            for b in range(4):
                                bg = tg * 4 + b
                                v.wait_ge(stp3.h, bg + 1)
                                a = v.tensor_copy(
                                    h2T[:, 8 * b:8 * b + 8, tl * 128:(tl + 1) * 128],
                                    ps_t3[bg % 2][:])
                                sev3.inc(a)
                        if c >= 1:
                            v.wait_ge(snm4.h, FTS * c)
                        for gi in range(G):
                            gg = c * G + gi
                            for tt in range(FTS):
                                v.wait_ge(sdnt.h, FTS * gg + (2 if tt < 2 else 4))
                                bank = 0 if tt < 2 else 1
                                a = v.tensor_scalar(
                                    out=f_raw[:, tt, gi * 256:(gi + 1) * 256],
                                    in0=ps_f[bank][:, (tt % 2) * 256:(tt % 2) * 256 + 256],
                                    scalar1=1.0, scalar2=None, op0=ALU.mult,
                                    op1=ALU.add, accum_out=s1p4[:, tt, gi:gi + 1])
                                sfev.inc(a)
                        v.wait_ge(sfev.h, FTS * G * (c + 1))
                        for tl in range(FTS):
                            tg = c * FTS + tl
                            a = v.reduce_sum(s1_4[:, tg:tg + 1], s1p4[:, tl, :],
                                             axis=AX.X)
                            fch_v.rt(v, a)
                            a = v.tensor_scalar(out=mean4[:, tg:tg + 1],
                                                in0=s1_4[:, tg:tg + 1],
                                                scalar1=1.0 / C, scalar2=None,
                                                op0=ALU.mult)
                            fch_v.rt(v, a)
                            v.wait_ge(ssq4.h, tg + 1)
                            v.tensor_mul(msq4[:, tg:tg + 1], mean4[:, tg:tg + 1],
                                         mean4[:, tg:tg + 1])
                            a = v.tensor_scalar(out=var4[:, tg:tg + 1],
                                                in0=s2_4[:, tg:tg + 1],
                                                scalar1=1.0 / C, scalar2=EPS,
                                                op0=ALU.mult, op1=ALU.add)
                            fch_v.rt(v, a)
                            a = v.tensor_sub(var4[:, tg:tg + 1], var4[:, tg:tg + 1],
                                             msq4[:, tg:tg + 1])
                            sv4.inc(a)
                        v.wait_ge(sr4.h, c + 1)
                        a = v.scalar_tensor_tensor(
                            out=nmr4[:, c * FTS:(c + 1) * FTS],
                            in0=mean4[:, c * FTS:(c + 1) * FTS], scalar=-1.0,
                            in1=rstd4[:, c * FTS:(c + 1) * FTS],
                            op0=ALU.mult, op1=ALU.mult)
                        sn4.inc(a)


                @blk.tensor
                def _(tn):
                    for c in range(NF):
                        for tl in range(FTS):
                            tg = c * FTS + tl
                            tn.wait_ge(sh2.h, tg + 1)
                            for b in range(4):
                                bg = tg * 4 + b
                                if bg >= 2:
                                    tn.wait_ge(sev3.h, bg - 1)
                                last = None
                                for j in range(8):
                                    ct = 8 * b + j
                                    last = tn.transpose(
                                        ps_t3[bg % 2][:, j, :],
                                        h2[tg % 2][:, ct * 128:(ct + 1) * 128],
                                        identG[:])
                                stp3.inc(last)
                        tn.wait_ge(sev3.h, 4 * FTS * (c + 1))
                        for gi in range(G + 1):
                            gg = c * G + gi
                            if gi < G:
                                tn.wait_ge(swf[gg % 4].h, 32 * (gg // 4 + 1))
                                for q in range(4):
                                    qg = 4 * gg + q
                                    if qg >= 2:
                                        tn.wait_ge(sgel.h, qg - 1)
                                    last = None
                                    for mi in range(2):
                                        mt = 2 * q + mi
                                        for kt_i in range(2):
                                            last = tn.matmul(
                                                ps_m[q % 2][:, mi, :],
                                                up_sb[gi % 4][:, kt_i, mt, :],
                                                h2T[:, 2 * gi + kt_i, :],
                                                start=(kt_i == 0), stop=(kt_i == 1))
                                    sup.inc(last)
                            if gi >= 1:
                                gp = gi - 1
                                ggp = c * G + gp
                                tn.wait_ge(sgel.h, 4 * ggp + 4)
                                for tt in range(FTS):
                                    bank = 0 if tt < 2 else 1
                                    if ggp >= 1:
                                        tn.wait_ge(sfev.h, FTS * (ggp - 1)
                                                   + (2 if tt < 2 else 4))
                                    last = None
                                    for mt in range(8):
                                        last = tn.matmul(
                                            ps_f[bank][:, (tt % 2) * 256:(tt % 2) * 256 + 256],
                                            midT[gp % 2][:, mt, tt * 128:(tt + 1) * 128],
                                            dn_sb[gp % 4][:, mt, :],
                                            start=(mt == 0), stop=(mt == 7))
                                    sdnt.inc(last)

    nc.finalize()
    return nc


def _pack(L0_attn_x, L0_attn_Wqkv, L0_attn_bqkv, L0_attn_Wo, L0_attn_bo,
          L0_ffn_up, L0_ffn_down, L0_ffn_perm, S=2048):
    perm = np.asarray(L0_ffn_perm)
    x = np.asarray(L0_attn_x, dtype=np.float32)[:, :S, :]
    x_p = np.ascontiguousarray(np.take(x, perm, axis=2))
    Wqkv = np.asarray(L0_attn_Wqkv, dtype=np.float32)
    Wqkv_p = np.ascontiguousarray(Wqkv[perm, :])
    bqkv = np.asarray(L0_attn_bqkv, dtype=np.float32)
    Wo = np.asarray(L0_attn_Wo, dtype=np.float32)
    Wo_p = np.ascontiguousarray(Wo[:, perm])
    bo_p = np.asarray(L0_attn_bo, dtype=np.float32)[perm]
    up = np.asarray(L0_ffn_up, dtype=np.float32)
    down = np.asarray(L0_ffn_down, dtype=np.float32)

    def wtile(w):
        return np.ascontiguousarray(
            w.reshape(CT, 128, 8, 128).transpose(2, 1, 0, 3).reshape(8, 128, C))

    shared = {
        "wq": wtile(Wqkv_p[:, 0:1024]),
        "wk": wtile(Wqkv_p[:, 1024:2048]),
        "wv": wtile(Wqkv_p[:, 2048:3072]),
        "bq": np.ascontiguousarray(bqkv[0:1024].reshape(8, 128).T),
        "bk": np.ascontiguousarray(bqkv[1024:2048].reshape(8, 128).T),
        "bv": np.ascontiguousarray(bqkv[2048:3072].reshape(8, 128).T),
        "wo": np.ascontiguousarray(Wo_p.reshape(8, 128, C)),
        "bo": np.ascontiguousarray(np.broadcast_to(bo_p, (128, C))),
        "upw": np.ascontiguousarray(
            up.reshape(G, 2, 128, 8, 128).transpose(0, 2, 1, 3, 4).reshape(G, 128, 2048)),
        "dnw": np.ascontiguousarray(
            down.reshape(G, 8, 128, 256).transpose(0, 2, 1, 3).reshape(G, 128, 2048)),
        "ident": np.eye(128, dtype=ml_dtypes.bfloat16),
    }
    in_maps = []
    for c in range(x_p.shape[0]):
        m = dict(shared)
        m["x"] = np.ascontiguousarray(x_p[c])
        in_maps.append(m)
    return in_maps, perm


def kernel(L0_attn_x, L0_attn_Wqkv, L0_attn_bqkv, L0_attn_Wo, L0_attn_bo,
           L0_ffn_up, L0_ffn_down, L0_ffn_perm):
    S = L0_attn_x.shape[1]
    in_maps, perm = _pack(L0_attn_x, L0_attn_Wqkv, L0_attn_bqkv, L0_attn_Wo,
                          L0_attn_bo, L0_ffn_up, L0_ffn_down, L0_ffn_perm, S=S)
    nc = build_program(S=S)
    res = run_bass_kernel_spmd(nc, in_maps, core_ids=list(range(len(in_maps))))
    out_p = np.stack([np.asarray(r["out"]) for r in res.results], axis=0)
    iperm = np.argsort(perm)
    return np.ascontiguousarray(np.take(out_p, iperm, axis=2)).astype(np.float32)


# version 07
# speedup vs baseline: 1.0542x; 1.0542x over previous
import numpy as np
import ml_dtypes
from contextlib import ExitStack

import concourse.bass as bass
import concourse.mybir as mybir
from concourse.bass_utils import run_bass_kernel_spmd

F32 = mybir.dt.float32
BF16 = mybir.dt.bfloat16
AF = mybir.ActivationFunctionType
ALU = mybir.AluOpType
AX = mybir.AxisListType

C = 4096
D = 1024
CT = C // 128
G = 16
EPS = 1e-5
ISQ = 1.0 / 32.0


class Sem:
    def __init__(self, ctx, nc, name):
        self.h = ctx.enter_context(nc.semaphore(name))
        self.n = 0

    def inc(self, instr, k=1):
        instr.then_inc(self.h, k)
        self.n += k

    def rt(self, eng, instr):
        instr.then_inc(self.h, 1)
        self.n += 1
        eng.wait_ge(self.h, self.n)


def build_program(S=2048):
    T = S // 128
    QTS = 16 if T % 16 == 0 else T
    NQ = T // QTS
    SC = (QTS * 128) // 512
    FTS = 4
    assert T % FTS == 0
    NF = T // FTS
    AB = (T + 7) // 8

    nc = bass.Bass("TRN2", target_bir_lowering=False)

    x_h = nc.dram_tensor("x", [S, C], F32, kind="ExternalInput").ap()
    wq_h = nc.dram_tensor("wq", [8, 128, C], F32, kind="ExternalInput").ap()
    wk_h = nc.dram_tensor("wk", [8, 128, C], F32, kind="ExternalInput").ap()
    wv_h = nc.dram_tensor("wv", [8, 128, C], F32, kind="ExternalInput").ap()
    bq_h = nc.dram_tensor("bq", [128, 8], F32, kind="ExternalInput").ap()
    bk_h = nc.dram_tensor("bk", [128, 8], F32, kind="ExternalInput").ap()
    bv_h = nc.dram_tensor("bv", [128, 8], F32, kind="ExternalInput").ap()
    wo_h = nc.dram_tensor("wo", [8, 128, C], F32, kind="ExternalInput").ap()
    bo_h = nc.dram_tensor("bo", [128, C], F32, kind="ExternalInput").ap()
    up_h = nc.dram_tensor("upw", [G, 128, 2048], F32, kind="ExternalInput").ap()
    dn_h = nc.dram_tensor("dnw", [G, 128, 2048], F32, kind="ExternalInput").ap()
    id_h = nc.dram_tensor("ident", [128, 128], BF16, kind="ExternalInput").ap()

    out_h = nc.dram_tensor("out", [S, C], F32, kind="ExternalOutput").ap()

    q_hh = nc.dram_tensor("q_scr", [8, 128, S], BF16, kind="Internal").ap()
    k_hh = nc.dram_tensor("k_scr", [8, 128, S], BF16, kind="Internal").ap()
    v_hh = nc.dram_tensor("v_scr", [8, 128, S], BF16, kind="Internal").ap()
    x1_hh = nc.dram_tensor("x1_scr", [T, 128, C], BF16, kind="Internal").ap()
    wo_bf = nc.dram_tensor("wo_bf", [8, 128, C], BF16, kind="Internal").ap()
    up_bf = nc.dram_tensor("up_bf", [G, 128, 2048], BF16, kind="Internal").ap()
    dn_bf = nc.dram_tensor("dn_bf", [G, 128, 2048], BF16, kind="Internal").ap()

    with ExitStack() as gctx:
        identG = gctx.enter_context(nc.sbuf_tensor("identG", [128, 128], BF16))

        s_id = Sem(gctx, nc, "s_id")
        sx = [Sem(gctx, nc, f"q_sx{i}") for i in range(3)]
        sbl = Sem(gctx, nc, "q_sbl")
        ssq = Sem(gctx, nc, "q_ssq")
        sv2 = Sem(gctx, nc, "q_sv2")
        srs = Sem(gctx, nc, "q_srs")
        snm = Sem(gctx, nc, "q_snm")
        sh1 = Sem(gctx, nc, "q_sh1")
        stp = Sem(gctx, nc, "q_stp")
        sev = Sem(gctx, nc, "q_sev")
        swl = [Sem(gctx, nc, f"q_swl{i}") for i in range(2)]
        smm = Sem(gctx, nc, "q_smm")
        sqe = Sem(gctx, nc, "q_sqe")
        sqs = [Sem(gctx, nc, f"q_sqs{i}") for i in range(4)]
        skt = Sem(gctx, nc, "a_skt")
        svt = Sem(gctx, nc, "a_svt")
        swo = Sem(gctx, nc, "a_swo")
        sq = [Sem(gctx, nc, f"a_sq{i}") for i in range(2)]
        sxr = [Sem(gctx, nc, f"a_sxr{i}") for i in range(2)]
        s_sc = Sem(gctx, nc, "a_ssc")
        s_mx = Sem(gctx, nc, "a_smx")
        s_ex = Sem(gctx, nc, "a_sex")
        s_at = Sem(gctx, nc, "a_sat")
        s_ae = Sem(gctx, nc, "a_sae")
        s_oa = Sem(gctx, nc, "a_soa")
        s_oe = Sem(gctx, nc, "a_soe")
        s_ot = Sem(gctx, nc, "a_sot")
        s_ote = Sem(gctx, nc, "a_sote")
        s_pj = Sem(gctx, nc, "a_spj")
        s_pje = Sem(gctx, nc, "a_spje")
        s_q2 = Sem(gctx, nc, "a_sq2")
        s_v2 = Sem(gctx, nc, "a_sv2")
        s_r2 = Sem(gctx, nc, "a_sr2")
        s_n2 = Sem(gctx, nc, "a_sn2")
        s_nm = Sem(gctx, nc, "a_snm")
        s_x1 = Sem(gctx, nc, "a_sx1")
        s_x1s = [Sem(gctx, nc, f"a_sx1s{i}") for i in range(2)]
        s_q3 = Sem(gctx, nc, "a_sq3")
        s_wc = Sem(gctx, nc, "q_swc")
        s_wcw = Sem(gctx, nc, "q_swcw")
        s_wol = Sem(gctx, nc, "a_swol")
        s_acc = [Sem(gctx, nc, f"f_sacc{i}") for i in range(2)]
        s_v3 = Sem(gctx, nc, "a_sv3")
        s_r3 = Sem(gctx, nc, "a_sr3")
        sxl = Sem(gctx, nc, "f_sxl")
        ssq3 = Sem(gctx, nc, "f_ssq3")
        sv3 = Sem(gctx, nc, "f_sv3")
        sr3 = Sem(gctx, nc, "f_sr3")
        sn3 = Sem(gctx, nc, "f_sn3")
        sh2 = Sem(gctx, nc, "f_sh2")
        stp3 = Sem(gctx, nc, "f_stp3")
        sev3 = Sem(gctx, nc, "f_sev3")
        swf = [Sem(gctx, nc, f"f_swf{i}") for i in range(4)]
        sup = Sem(gctx, nc, "f_sup")
        sgel = Sem(gctx, nc, "f_sgel")
        sdnt = Sem(gctx, nc, "f_sdnt")
        sfev = Sem(gctx, nc, "f_sfev")
        ssq4 = Sem(gctx, nc, "f_ssq4")
        sv4 = Sem(gctx, nc, "f_sv4")
        sr4 = Sem(gctx, nc, "f_sr4")
        sn4 = Sem(gctx, nc, "f_sn4")
        snm4 = Sem(gctx, nc, "f_snm4")
        sadd = Sem(gctx, nc, "f_sadd")
        sost = [Sem(gctx, nc, f"f_sost{i}") for i in range(2)]
        s1_3 = gctx.enter_context(nc.sbuf_tensor("s1_3", [128, T], F32))
        s2_3 = gctx.enter_context(nc.sbuf_tensor("s2_3", [128, T], F32))
        mean3 = gctx.enter_context(nc.sbuf_tensor("mean3", [128, T], F32))
        var3 = gctx.enter_context(nc.sbuf_tensor("var3", [128, T], F32))
        msq3 = gctx.enter_context(nc.sbuf_tensor("msq3", [128, T], F32))
        lnv3 = gctx.enter_context(nc.sbuf_tensor("lnv3", [128, T], F32))
        rstd3 = gctx.enter_context(nc.sbuf_tensor("rstd3", [128, T], F32))
        nmr3 = gctx.enter_context(nc.sbuf_tensor("nmr3", [128, T], F32))
        qch_v = Sem(gctx, nc, "qch_v")
        qch_s = Sem(gctx, nc, "qch_s")
        ach_v = Sem(gctx, nc, "ach_v")
        ach_s = Sem(gctx, nc, "ach_s")
        fch_v = Sem(gctx, nc, "fch_v")
        fch_s = Sem(gctx, nc, "fch_s")

        with ExitStack() as ctx:
            E = ctx.enter_context
            xb = [E(nc.sbuf_tensor(f"xb{i}", [128, C], BF16)) for i in range(3)]
            h1 = [E(nc.sbuf_tensor(f"h1_{i}", [128, C], BF16)) for i in range(2)]
            h1T = E(nc.sbuf_tensor("h1T", [128, CT, QTS * 128], BF16))
            wbuf = [E(nc.sbuf_tensor(f"wbuf{i}", [128, CT, 128], BF16)) for i in range(2)]
            vscr = E(nc.sbuf_tensor("vscr", [128, C], BF16))
            vscr2 = E(nc.sbuf_tensor("vscr2", [128, C], BF16))
            qev = [E(nc.sbuf_tensor(f"qev{i}", [128, 512], BF16)) for i in range(4)]
            bq_s = E(nc.sbuf_tensor("bq_s", [128, 8], F32))
            bk_s = E(nc.sbuf_tensor("bk_s", [128, 8], F32))
            bv_s = E(nc.sbuf_tensor("bv_s", [128, 8], F32))
            s1q = E(nc.sbuf_tensor("s1q", [128, T], F32))
            s2q = E(nc.sbuf_tensor("s2q", [128, T], F32))
            meanq = E(nc.sbuf_tensor("meanq", [128, T], F32))
            varq = E(nc.sbuf_tensor("varq", [128, T], F32))
            msqq = E(nc.sbuf_tensor("msqq", [128, T], F32))
            lnvq = E(nc.sbuf_tensor("lnvq", [128, T], F32))
            rstdq = E(nc.sbuf_tensor("rstdq", [128, T], F32))
            nmrq = E(nc.sbuf_tensor("nmrq", [128, T], F32))
            ps_t = [E(nc.psum_tensor(f"ps_t{i}", [128, 8, 128], BF16)) for i in range(2)]
            ps_q = [E(nc.psum_tensor(f"ps_q{i}", [128, 512], F32)) for i in range(4)]

            with nc.Block() as blk:
                @blk.gpsimd
                def _(g):
                    g.dma_start(out=identG[:], in_=id_h[:]).then_inc(s_id.h, 16)
                    s_id.n += 16
                    for name, dst, src in (("bq", bq_s, bq_h), ("bk", bk_s, bk_h),
                                           ("bv", bv_s, bv_h)):
                        d = g.dma_start(out=dst[:], in_=src[:])
                        sbl.inc(d, 16)
                    for ch in range(NQ):
                        for i in range(QTS):
                            t = ch * QTS + i
                            if t >= 3:
                                g.wait_ge(sh1.h, t - 2)
                                g.wait_ge(sx[t % 3].h, 16 * (t // 3))
                            d = g.dma_start(out=xb[t % 3][:],
                                            in_=x_h[t * 128:(t + 1) * 128, :])
                            sx[t % 3].inc(d, 16)
                        for f in range(24):
                            fg = ch * 24 + f
                            if fg >= 2:
                                g.wait_ge(smm.h, SC * (fg - 1))
                                g.wait_ge(swl[fg % 2].h, 16 * (fg // 2))
                            src = wq_h[f] if f < 8 else (
                                wk_h[f - 8] if f < 16 else wv_h[f - 16])
                            d = g.dma_start(out=wbuf[fg % 2][:], in_=src)
                            swl[fg % 2].inc(d, 16)
                    for dt in range(8):
                        d = g.dma_start(out=wo_bf[dt], in_=wo_h[dt])
                        s_wcw.inc(d, 16)
                    for gi in range(G):
                        d = g.dma_start(out=up_bf[gi], in_=up_h[gi])
                        s_wc.inc(d, 16)
                        d = g.dma_start(out=dn_bf[gi], in_=dn_h[gi])
                        s_wc.inc(d, 16)

                @blk.sync
                def _(sy):
                    for ch in range(NQ):
                        for f in range(24):
                            for s_i in range(SC):
                                e = (ch * 24 + f) * SC + s_i
                                sy.wait_ge(sqe.h, e + 1)
                                if e >= 4:
                                    sy.wait_ge(sqs[e % 4].h, 16 * (e // 4))
                                off = ch * QTS * 128 + s_i * 512
                                if f < 8:
                                    dst = q_hh[f, :, off:off + 512]
                                elif f < 16:
                                    dst = k_hh[f - 8, :, off:off + 512]
                                else:
                                    dst = v_hh[f - 16, :, off:off + 512]
                                d = sy.dma_start(out=dst, in_=qev[e % 4][:])
                                sqs[e % 4].inc(d, 16)
                    for ss in sqs:
                        sy.wait_ge(ss.h, ss.n)

                @blk.scalar
                def _(s):
                    s.wait_ge(sbl.h, 48)
                    for ch in range(NQ):
                        for i in range(QTS):
                            t = ch * QTS + i
                            s.wait_ge(sx[t % 3].h, 16 * (t // 3 + 1))
                            a = s.activation(vscr[:], xb[t % 3][:], AF.Square,
                                             accum_out=s2q[:, t:t + 1])
                            ssq.inc(a)
                            s.wait_ge(sv2.h, t + 1)
                            a = s.activation(lnvq[:, t:t + 1], varq[:, t:t + 1], AF.Ln)
                            qch_s.rt(s, a)
                            a = s.activation(rstdq[:, t:t + 1], lnvq[:, t:t + 1],
                                             AF.Exp, scale=-0.5)
                            srs.inc(a)
                            s.wait_ge(snm.h, t + 1)
                            if t >= 2:
                                s.wait_ge(stp.h, 4 * (t - 1))
                            a = s.activation(h1[t % 2][:], xb[t % 3][:], AF.Identity,
                                             bias=nmrq[:, t:t + 1],
                                             scale=rstdq[:, t:t + 1])
                            sh1.inc(a)
                        for f in range(24):
                            for s_i in range(SC):
                                grp = (ch * 24 + f) * SC + s_i
                                s.wait_ge(smm.h, grp + 1)
                                if grp >= 4:
                                    s.wait_ge(sqs[grp % 4].h, 16 * (grp // 4))
                                if f < 8:
                                    bias = bq_s[:, f:f + 1]
                                elif f < 16:
                                    bias = bk_s[:, f - 8:f - 7]
                                else:
                                    bias = bv_s[:, f - 16:f - 15]
                                a = s.activation(qev[grp % 4][:], ps_q[grp % 4][:],
                                                 AF.Identity, bias=bias)
                                sqe.inc(a)

                @blk.vector
                def _(v):
                    for t in range(T):
                        v.wait_ge(sx[t % 3].h, 16 * (t // 3 + 1))
                        a = v.tensor_scalar(out=vscr2[:], in0=xb[t % 3][:],
                                            scalar1=1.0, scalar2=None,
                                            op0=ALU.mult, op1=ALU.add,
                                            accum_out=s1q[:, t:t + 1])
                        qch_v.rt(v, a)
                        a = v.tensor_scalar(out=meanq[:, t:t + 1], in0=s1q[:, t:t + 1],
                                            scalar1=1.0 / C, scalar2=None, op0=ALU.mult)
                        qch_v.rt(v, a)
                        v.tensor_mul(msqq[:, t:t + 1], meanq[:, t:t + 1],
                                     meanq[:, t:t + 1])
                        v.wait_ge(ssq.h, t + 1)
                        a = v.tensor_scalar(out=varq[:, t:t + 1], in0=s2q[:, t:t + 1],
                                            scalar1=1.0 / C, scalar2=EPS,
                                            op0=ALU.mult, op1=ALU.add)
                        qch_v.rt(v, a)
                        a = v.tensor_sub(varq[:, t:t + 1], varq[:, t:t + 1],
                                         msqq[:, t:t + 1])
                        sv2.inc(a)
                        v.wait_ge(srs.h, t + 1)
                        a = v.scalar_tensor_tensor(out=nmrq[:, t:t + 1],
                                                   in0=meanq[:, t:t + 1],
                                                   scalar=-1.0, in1=rstdq[:, t:t + 1],
                                                   op0=ALU.mult, op1=ALU.mult)
                        snm.inc(a)
                        i = t % QTS
                        for b in range(4):
                            bg = t * 4 + b
                            v.wait_ge(stp.h, bg + 1)
                            a = v.tensor_copy(
                                h1T[:, 8 * b:8 * b + 8, i * 128:(i + 1) * 128],
                                ps_t[bg % 2][:])
                            sev.inc(a)

                @blk.tensor
                def _(tn):
                    tn.wait_ge(s_id.h, 16)
                    for ch in range(NQ):
                        for i in range(QTS):
                            t = ch * QTS + i
                            tn.wait_ge(sh1.h, t + 1)
                            for b in range(4):
                                bg = t * 4 + b
                                if bg >= 2:
                                    tn.wait_ge(sev.h, bg - 1)
                                last = None
                                for j in range(8):
                                    ct = 8 * b + j
                                    last = tn.transpose(
                                        ps_t[bg % 2][:, j, :],
                                        h1[t % 2][:, ct * 128:(ct + 1) * 128],
                                        identG[:])
                                stp.inc(last)
                        for f in range(24):
                            fg = ch * 24 + f
                            tn.wait_ge(swl[fg % 2].h, 16 * (fg // 2 + 1))
                            for s_i in range(SC):
                                grp = fg * SC + s_i
                                need = ch * QTS + min(4 * (s_i + 1), QTS)
                                tn.wait_ge(sev.h, 4 * need)
                                if grp >= 4:
                                    tn.wait_ge(sqe.h, grp - 3)
                                last = None
                                for ct in range(CT):
                                    last = tn.matmul(
                                        ps_q[grp % 4][:],
                                        wbuf[fg % 2][:, ct, :],
                                        h1T[:, ct, s_i * 512:(s_i + 1) * 512],
                                        start=(ct == 0), stop=(ct == CT - 1))
                                smm.inc(last)

        with ExitStack() as ctx:
            E = ctx.enter_context
            kT = E(nc.sbuf_tensor("kT", [128, 8, S], BF16))
            v_sb = E(nc.sbuf_tensor("v_sb", [128, T, D], BF16))
            wo_sb = E(nc.sbuf_tensor("wo_sb", [128, 8, C], BF16))
            bo_sb = E(nc.sbuf_tensor("bo_sb", [128, C], BF16))
            qt = [E(nc.sbuf_tensor(f"qt{i}", [128, 8, 128], BF16)) for i in range(2)]
            probs = E(nc.sbuf_tensor("probs", [128, S], BF16))
            attnT = E(nc.sbuf_tensor("attnT", [128, T, 128], BF16))
            o_sb = E(nc.sbuf_tensor("o_sb", [128, D], BF16))
            oT = [E(nc.sbuf_tensor(f"oT{i}", [128, 8, 128], BF16)) for i in range(2)]
            pb2 = [E(nc.sbuf_tensor(f"pb2_{i}", [128, C], BF16)) for i in range(2)]
            xr = [E(nc.sbuf_tensor(f"xr{i}", [128, C], BF16)) for i in range(2)]
            x1b = [E(nc.sbuf_tensor(f"x1b{i}", [128, C], BF16)) for i in range(2)]
            negmax = E(nc.sbuf_tensor("negmax", [128, T], F32))
            nmax32 = E(nc.sbuf_tensor("nmax32", [128, T], F32))
            sumexp = E(nc.sbuf_tensor("sumexp", [128, T], F32))
            rs_t = E(nc.sbuf_tensor("rs_t", [128, T], F32))
            s1p = E(nc.sbuf_tensor("s1p", [128, 8], F32))
            s1a = E(nc.sbuf_tensor("s1a", [128, T], F32))
            s2a = E(nc.sbuf_tensor("s2a", [128, T], F32))
            meana = E(nc.sbuf_tensor("meana", [128, T], F32))
            vara = E(nc.sbuf_tensor("vara", [128, T], F32))
            msqa = E(nc.sbuf_tensor("msqa", [128, T], F32))
            lnva = E(nc.sbuf_tensor("lnva", [128, T], F32))
            rstda = E(nc.sbuf_tensor("rstda", [128, T], F32))
            nmra = E(nc.sbuf_tensor("nmra", [128, T], F32))
            ps_s = E(nc.psum_tensor("ps_s", [128, S], F32))
            ps_at = E(nc.psum_tensor("ps_at", [128, 8, 128], BF16))
            ps_p = [E(nc.psum_tensor(f"ps_p{i}", [128, 512], F32))
                    for i in range(3)]
            if S >= 2 * D:
                ps_o = ps_s[:, S - D:S]
            else:
                ps_o = E(nc.psum_tensor("ps_o", [128, D], F32))

            with nc.Block() as blk:
                @blk.sync
                def _(sy):
                    for dt in range(8):
                        d = sy.dma_start(out=kT[:, dt, :], in_=k_hh[dt])
                        skt.inc(d, 16)
                    sy.wait_ge(s_wcw.h, 128)
                    for dt in range(8):
                        d = sy.dma_start(out=wo_sb[:, dt, :], in_=wo_bf[dt])
                        s_wol.inc(d, 16)
                    for kt_i in range(T):
                        src = v_hh[:, :, kt_i * 128:(kt_i + 1) * 128].rearrange(
                            "a p m -> (a p) m")
                        d = sy.dma_start_transpose(v_sb[:, kt_i, :], src)
                        svt.inc(d, 16)
                    for t in range(T):
                        if t >= 2:
                            sy.wait_ge(s_sc.h, t - 1)
                            sy.wait_ge(sq[t % 2].h, 16 * (t // 2))
                        src = q_hh[:, :, t * 128:(t + 1) * 128].rearrange(
                            "a p m -> p a m")
                        d = sy.dma_start(out=qt[t % 2][:], in_=src)
                        sq[t % 2].inc(d, 16)

                @blk.gpsimd
                def _(g):
                    d = g.dma_start(out=bo_sb[:], in_=bo_h[:])
                    swo.inc(d, 16)
                    for t in range(T):
                        if t >= 2:
                            g.wait_ge(s_x1.h, t - 1)
                            g.wait_ge(sxr[t % 2].h, 16 * (t // 2))
                        d = g.dma_start(out=xr[t % 2][:],
                                        in_=x_h[t * 128:(t + 1) * 128, :])
                        sxr[t % 2].inc(d, 16)
                        if t >= 1:
                            j = t - 1
                            g.wait_ge(s_x1.h, t)
                            if j >= 2:
                                g.wait_ge(s_x1s[j % 2].h, 16 * (j // 2))
                            d = g.dma_start(out=x1_hh[j], in_=x1b[j % 2][:])
                            s_x1s[j % 2].inc(d, 16)
                    j = T - 1
                    g.wait_ge(s_x1.h, T)
                    g.wait_ge(s_x1s[j % 2].h, 16 * (j // 2))
                    d = g.dma_start(out=x1_hh[j], in_=x1b[j % 2][:])
                    s_x1s[j % 2].inc(d, 16)
                    for ss in s_x1s:
                        g.wait_ge(ss.h, ss.n)

                @blk.tensor
                def _(tn):
                    for t in range(T + 1):
                        if t >= 1:
                            tn.wait_ge(s_oe.h, t)
                            tn.wait_ge(s_ae.h, AB * t)
                            if t >= 2:
                                tn.wait_ge(s_ote.h, t - 1)
                            last = None
                            for dt in range(8):
                                last = tn.transpose(
                                    ps_at[:, dt, :],
                                    o_sb[:, dt * 128:(dt + 1) * 128], identG[:])
                            s_ot.inc(last)
                        if t < T:
                            tn.wait_ge(sq[t % 2].h, 16 * (t // 2 + 1))
                            if t == 0:
                                tn.wait_ge(skt.h, skt.n)
                            if t >= 1:
                                tn.wait_ge(s_ex.h, t)
                                if S >= 2 * D:
                                    tn.wait_ge(s_oe.h, t)
                            last = None
                            for s_i in range(S // 512):
                                for dt in range(8):
                                    last = tn.matmul(
                                        ps_s[:, s_i * 512:(s_i + 1) * 512],
                                        qt[t % 2][:, dt, :],
                                        kT[:, dt, s_i * 512:(s_i + 1) * 512],
                                        start=(dt == 0), stop=(dt == 7))
                            s_sc.inc(last)
                        if t >= 1:
                            tn.wait_ge(s_ote.h, t)
                            if t == 1:
                                tn.wait_ge(swo.h, swo.n)
                                tn.wait_ge(s_wol.h, s_wol.n)
                            for cq in range(8):
                                pg = 8 * (t - 1) + cq
                                if pg >= 3:
                                    tn.wait_ge(s_pje.h, pg - 2)
                                last = None
                                for dt in range(8):
                                    last = tn.matmul(
                                        ps_p[pg % 3][:],
                                        oT[(t - 1) % 2][:, dt, :],
                                        wo_sb[:, dt, cq * 512:(cq + 1) * 512],
                                        start=(dt == 0), stop=(dt == 7))
                                s_pj.inc(last)
                        if t < T:
                            tn.wait_ge(s_ex.h, t + 1)
                            for b in range(AB):
                                bg = AB * t + b
                                if bg >= 1:
                                    tn.wait_ge(s_ae.h, bg)
                                nb = min(8, T - 8 * b)
                                last = None
                                for j in range(nb):
                                    kt_i = 8 * b + j
                                    last = tn.transpose(
                                        ps_at[:, j, :],
                                        probs[:, kt_i * 128:(kt_i + 1) * 128],
                                        identG[:])
                                s_at.inc(last)
                            if t == 0:
                                tn.wait_ge(svt.h, svt.n)
                            if t >= 1:
                                tn.wait_ge(s_oe.h, t)
                            last1 = None
                            for kt_i in range(T):
                                if kt_i % 8 == 0:
                                    tn.wait_ge(s_ae.h, AB * t + kt_i // 8 + 1)
                                src = attnT[:, kt_i, :]
                                tn.matmul(ps_o[:, 0:512], src,
                                          v_sb[:, kt_i, 0:512],
                                          start=(kt_i == 0), stop=(kt_i == T - 1),
                                          skip_group_check=True)
                                last1 = tn.matmul(ps_o[:, 512:1024], src,
                                                  v_sb[:, kt_i, 512:1024],
                                                  start=(kt_i == 0),
                                                  stop=(kt_i == T - 1),
                                                  skip_group_check=True)
                            s_oa.inc(last1)

                @blk.vector
                def _(v):
                    for t in range(T + 1):
                        if t >= 1:
                            v.wait_ge(s_ot.h, t)
                            if t >= 3:
                                v.wait_ge(s_pj.h, 8 * (t - 2))
                            a = v.tensor_copy(oT[(t - 1) % 2][:], ps_at[:])
                            s_ote.inc(a)
                        if t < T:
                            v.wait_ge(s_sc.h, t + 1)
                            a = v.reduce_max(negmax[:, t:t + 1], ps_s[:], axis=AX.X,
                                             negate=True)
                            ach_v.rt(v, a)
                            a = v.tensor_scalar(out=nmax32[:, t:t + 1],
                                                in0=negmax[:, t:t + 1],
                                                scalar1=ISQ, scalar2=None,
                                                op0=ALU.mult)
                            s_mx.inc(a)
                        if t >= 1:
                            for cq in range(8):
                                pg = 8 * (t - 1) + cq
                                v.wait_ge(s_pj.h, pg + 1)
                                if cq == 0 and t >= 3:
                                    v.wait_ge(s_nm.h, t - 2)
                                a = v.scalar_tensor_tensor(
                                    out=pb2[(t - 1) % 2][:, cq * 512:(cq + 1) * 512],
                                    in0=ps_p[pg % 3][:], scalar=rs_t[:, t - 1:t],
                                    in1=bo_sb[:, cq * 512:(cq + 1) * 512],
                                    op0=ALU.mult, op1=ALU.add,
                                    accum_out=s1p[:, cq:cq + 1])
                                s_pje.inc(a)
                        if t < T:
                            for b in range(AB):
                                v.wait_ge(s_at.h, AB * t + b + 1)
                                if t >= 1 and b == 0:
                                    v.wait_ge(s_oa.h, t)
                                nb = min(8, T - 8 * b)
                                a = v.tensor_copy(attnT[:, 8 * b:8 * b + nb, :],
                                                  ps_at[:, 0:nb, :])
                                s_ae.inc(a)
                            v.wait_ge(s_ex.h, t + 1)
                            a = v.reciprocal(rs_t[:, t:t + 1], sumexp[:, t:t + 1])
                            ach_v.rt(v, a)
                        if t >= 1:
                            v.wait_ge(s_pje.h, 8 * t)
                            a = v.reduce_sum(s1a[:, t - 1:t], s1p[:], axis=AX.X)
                            ach_v.rt(v, a)
                            a = v.tensor_scalar(out=meana[:, t - 1:t],
                                                in0=s1a[:, t - 1:t],
                                                scalar1=1.0 / C, scalar2=None,
                                                op0=ALU.mult)
                            ach_v.rt(v, a)
                            v.wait_ge(s_q2.h, t)
                            v.tensor_mul(msqa[:, t - 1:t], meana[:, t - 1:t],
                                         meana[:, t - 1:t])
                            a = v.tensor_scalar(out=vara[:, t - 1:t],
                                                in0=s2a[:, t - 1:t],
                                                scalar1=1.0 / C, scalar2=EPS,
                                                op0=ALU.mult, op1=ALU.add)
                            ach_v.rt(v, a)
                            a = v.tensor_sub(vara[:, t - 1:t], vara[:, t - 1:t],
                                             msqa[:, t - 1:t])
                            s_v2.inc(a)
                        if t >= 1:
                            v.wait_ge(s_r2.h, t)
                            a = v.scalar_tensor_tensor(
                                out=nmra[:, t - 1:t], in0=meana[:, t - 1:t],
                                scalar=-1.0, in1=rstda[:, t - 1:t],
                                op0=ALU.mult, op1=ALU.mult)
                            s_n2.inc(a)
                            v.wait_ge(s_nm.h, t)
                            v.wait_ge(sxr[(t - 1) % 2].h, 16 * ((t - 1) // 2 + 1))
                            a = v.tensor_add(x1b[(t - 1) % 2][:],
                                             x1b[(t - 1) % 2][:],
                                             xr[(t - 1) % 2][:])
                            s_x1.inc(a)
                            v.wait_ge(s_x1.h, t)
                            a = v.reduce_sum(s1_3[:, t - 1:t],
                                             x1b[(t - 1) % 2][:], axis=AX.X)
                            ach_v.rt(v, a)
                            a = v.tensor_scalar(out=mean3[:, t - 1:t],
                                                in0=s1_3[:, t - 1:t],
                                                scalar1=1.0 / C, scalar2=None,
                                                op0=ALU.mult)
                            ach_v.rt(v, a)
                            v.wait_ge(s_q3.h, t)
                            v.tensor_mul(msq3[:, t - 1:t], mean3[:, t - 1:t],
                                         mean3[:, t - 1:t])
                            a = v.tensor_scalar(out=var3[:, t - 1:t],
                                                in0=s2_3[:, t - 1:t],
                                                scalar1=1.0 / C, scalar2=EPS,
                                                op0=ALU.mult, op1=ALU.add)
                            ach_v.rt(v, a)
                            a = v.tensor_sub(var3[:, t - 1:t], var3[:, t - 1:t],
                                             msq3[:, t - 1:t])
                            s_v3.inc(a)
                            v.wait_ge(s_r3.h, t)
                            a = v.scalar_tensor_tensor(
                                out=nmr3[:, t - 1:t], in0=mean3[:, t - 1:t],
                                scalar=-1.0, in1=rstd3[:, t - 1:t],
                                op0=ALU.mult, op1=ALU.mult)
                            ach_v.rt(v, a)

                @blk.scalar
                def _(s):
                    for t in range(T + 1):
                        if t < T:
                            s.wait_ge(s_mx.h, t + 1)
                            if t >= 1:
                                s.wait_ge(s_at.h, AB * t)
                            a = s.activation(probs[:], ps_s[:], AF.Exp,
                                             bias=nmax32[:, t:t + 1], scale=ISQ,
                                             accum_out=sumexp[:, t:t + 1])
                            s_ex.inc(a)
                        if t < T:
                            s.wait_ge(s_oa.h, t + 1)
                            s.wait_ge(s_ot.h, t)
                            a = s.activation(o_sb[:], ps_o[:], AF.Identity)
                            s_oe.inc(a)
                        if t >= 1:
                            s.wait_ge(s_pje.h, 8 * t)
                            if t >= 3:
                                s.wait_ge(s_x1s[(t - 1) % 2].h,
                                          16 * ((t - 3) // 2 + 1))
                            a = s.activation(x1b[(t - 1) % 2][:],
                                             pb2[(t - 1) % 2][:],
                                             AF.Square, accum_out=s2a[:, t - 1:t])
                            s_q2.inc(a)
                        if t >= 1:
                            s.wait_ge(s_v2.h, t)
                            a = s.activation(lnva[:, t - 1:t], vara[:, t - 1:t], AF.Ln)
                            ach_s.rt(s, a)
                            a = s.activation(rstda[:, t - 1:t], lnva[:, t - 1:t],
                                             AF.Exp, scale=-0.5)
                            s_r2.inc(a)
                            s.wait_ge(s_n2.h, t)
                            a = s.activation(x1b[(t - 1) % 2][:],
                                             pb2[(t - 1) % 2][:], AF.Identity,
                                             bias=nmra[:, t - 1:t],
                                             scale=rstda[:, t - 1:t])
                            s_nm.inc(a)
                            s.wait_ge(s_x1.h, t)
                            a = s.activation(pb2[(t - 1) % 2][:],
                                             x1b[(t - 1) % 2][:], AF.Square,
                                             accum_out=s2_3[:, t - 1:t])
                            s_q3.inc(a)
                            s.wait_ge(s_v3.h, t)
                            a = s.activation(lnv3[:, t - 1:t], var3[:, t - 1:t],
                                             AF.Ln)
                            ach_s.rt(s, a)
                            a = s.activation(rstd3[:, t - 1:t], lnv3[:, t - 1:t],
                                             AF.Exp, scale=-0.5)
                            s_r3.inc(a)

        with ExitStack() as ctx:
            E = ctx.enter_context
            x1c = E(nc.sbuf_tensor("x1c", [128, FTS, C], BF16))
            h2 = [E(nc.sbuf_tensor(f"h2_{i}", [128, C], BF16)) for i in range(2)]
            h2T = E(nc.sbuf_tensor("h2T", [128, CT, FTS * 128], BF16))
            up_sb = [E(nc.sbuf_tensor(f"up_sb{i}", [128, 2, 8, 128], BF16))
                     for i in range(4)]
            dn_sb = [E(nc.sbuf_tensor(f"dn_sb{i}", [128, 8, 256], BF16))
                     for i in range(4)]
            midT = [E(nc.sbuf_tensor(f"midT{i}", [128, 8, FTS * 128], BF16))
                    for i in range(2)]
            f_raw = E(nc.sbuf_tensor("f_raw", [128, FTS, C], BF16))
            scr3 = E(nc.sbuf_tensor("scr3", [128, C], BF16))
            outb = [E(nc.sbuf_tensor(f"outb{i}", [128, C], F32)) for i in range(2)]
            s1p4 = E(nc.sbuf_tensor("s1p4", [128, FTS, G], F32))
            s1_4 = E(nc.sbuf_tensor("s1_4", [128, T], F32))
            s2_4 = E(nc.sbuf_tensor("s2_4", [128, T], F32))
            mean4 = E(nc.sbuf_tensor("mean4", [128, T], F32))
            var4 = E(nc.sbuf_tensor("var4", [128, T], F32))
            msq4 = E(nc.sbuf_tensor("msq4", [128, T], F32))
            lnv4 = E(nc.sbuf_tensor("lnv4", [128, T], F32))
            rstd4 = E(nc.sbuf_tensor("rstd4", [128, T], F32))
            nmr4 = E(nc.sbuf_tensor("nmr4", [128, T], F32))
            ps_m = [E(nc.psum_tensor(f"ps_m{i}", [128, 2, 512], F32))
                    for i in range(2)]
            ps_f = [E(nc.psum_tensor(f"ps_f{i}", [128, 512], F32)) for i in range(2)]
            ps_t3 = [E(nc.psum_tensor(f"ps_t3{i}", [128, 8, 128], BF16))
                     for i in range(2)]

            with nc.Block() as blk:
                @blk.sync
                def _(sy):
                    for c in range(NF + 1):
                        if c < NF:
                            if c >= 1:
                                sy.wait_ge(sh2.h, FTS * c)
                                sy.wait_ge(sxl.h, 16 * c)
                            src = x1_hh[c * FTS:(c + 1) * FTS].rearrange(
                                "a p m -> p a m")
                            d = sy.dma_start(out=x1c[:], in_=src)
                            sxl.inc(d, 16)
                            for gi in range(4):
                                gg = c * G + gi
                                if gg == 0 and c == 0:
                                    sy.wait_ge(s_wc.h, 512)
                                if gg >= 4:
                                    sy.wait_ge(sdnt.h, FTS * (gg - 3))
                                    sy.wait_ge(swf[gg % 4].h, 32 * (gg // 4))
                                d = sy.dma_start(out=up_sb[gi % 4][:],
                                                 in_=up_bf[gi])
                                swf[gg % 4].inc(d, 16)
                                d = sy.dma_start(out=dn_sb[gi % 4][:],
                                                 in_=dn_bf[gi])
                                swf[gg % 4].inc(d, 16)
                        if c >= 1:
                            cp = c - 1
                            for tl in range(FTS):
                                tg = cp * FTS + tl
                                sy.wait_ge(s_acc[tg % 2].h, 32 * (tg // 2 + 1))
                                if tg >= 2:
                                    sy.wait_ge(sost[tg % 2].h, 16 * (tg // 2))
                                d = sy.dma_start(
                                    out=out_h[tg * 128:(tg + 1) * 128, :],
                                    in_=outb[tl % 2][:])
                                sost[tg % 2].inc(d, 16)
                        if c < NF:
                            for gi in range(4, G):
                                gg = c * G + gi
                                sy.wait_ge(sdnt.h, FTS * (gg - 3))
                                sy.wait_ge(swf[gg % 4].h, 32 * (gg // 4))
                                d = sy.dma_start(out=up_sb[gi % 4][:],
                                                 in_=up_bf[gi])
                                swf[gg % 4].inc(d, 16)
                                d = sy.dma_start(out=dn_sb[gi % 4][:],
                                                 in_=dn_bf[gi])
                                swf[gg % 4].inc(d, 16)
                    for ss in sost:
                        sy.wait_ge(ss.h, ss.n)

                @blk.gpsimd
                def _(g):
                    for c in range(NF):
                        for tl in range(FTS):
                            tg = c * FTS + tl
                            g.wait_ge(snm4.h, tg + 1)
                            if tg >= 2:
                                g.wait_ge(s_acc[tg % 2].h, 32 * (tg // 2))
                            d = g.dma_start(out=outb[tl % 2][:, 0:2048],
                                            in_=x1_hh[tg][:, 0:2048],
                                            accum_op=ALU.add)
                            s_acc[tg % 2].inc(d, 16)
                            d = g.dma_start(out=outb[tl % 2][:, 2048:C],
                                            in_=x1_hh[tg][:, 2048:C],
                                            accum_op=ALU.add)
                            s_acc[tg % 2].inc(d, 16)


                @blk.scalar
                def _(s):
                    for c in range(NF):
                        s.wait_ge(sxl.h, 16 * (c + 1))
                        for tl in range(FTS):
                            tg = c * FTS + tl
                            if tg >= 2:
                                s.wait_ge(stp3.h, 4 * (tg - 1))
                            a = s.activation(h2[tg % 2][:], x1c[:, tl, :], AF.Identity,
                                             bias=nmr3[:, tg:tg + 1],
                                             scale=rstd3[:, tg:tg + 1])
                            sh2.inc(a)
                        for gi in range(G):
                            gg = c * G + gi
                            for q in range(4):
                                s.wait_ge(sup.h, 4 * gg + q + 1)
                                if gg >= 2 and q == 0:
                                    s.wait_ge(sdnt.h, FTS * (gg - 1))
                                a = s.activation(midT[gi % 2][:, 2 * q:2 * q + 2, :],
                                                 ps_m[q % 2][:], AF.Gelu)
                                sgel.inc(a)
                        s.wait_ge(sfev.h, FTS * G * (c + 1))
                        for tl in range(FTS):
                            tg = c * FTS + tl
                            if tg >= 1:
                                s.wait_ge(ssq4.h, tg)
                            a = s.activation(scr3[:], f_raw[:, tl, :],
                                             AF.Square,
                                             accum_out=s2_4[:, tg:tg + 1])
                            ssq4.inc(a)
                        s.wait_ge(sv4.h, FTS * (c + 1))
                        a = s.activation(lnv4[:, c * FTS:(c + 1) * FTS],
                                         var4[:, c * FTS:(c + 1) * FTS], AF.Ln)
                        fch_s.rt(s, a)
                        a = s.activation(rstd4[:, c * FTS:(c + 1) * FTS],
                                         lnv4[:, c * FTS:(c + 1) * FTS],
                                         AF.Exp, scale=-0.5)
                        sr4.inc(a)
                        s.wait_ge(sn4.h, c + 1)
                        for tl in range(FTS):
                            tg = c * FTS + tl
                            if tg >= 2:
                                s.wait_ge(sost[tg % 2].h, 16 * (tg // 2))
                            a = s.activation(outb[tl % 2][:], f_raw[:, tl, :],
                                             AF.Identity, bias=nmr4[:, tg:tg + 1],
                                             scale=rstd4[:, tg:tg + 1])
                            snm4.inc(a)

                @blk.vector
                def _(v):
                    for c in range(NF):
                        for tl in range(FTS):
                            tg = c * FTS + tl
                            for b in range(4):
                                bg = tg * 4 + b
                                v.wait_ge(stp3.h, bg + 1)
                                a = v.tensor_copy(
                                    h2T[:, 8 * b:8 * b + 8, tl * 128:(tl + 1) * 128],
                                    ps_t3[bg % 2][:])
                                sev3.inc(a)
                        if c >= 1:
                            v.wait_ge(snm4.h, FTS * c)
                        for gi in range(G):
                            gg = c * G + gi
                            for tt in range(FTS):
                                v.wait_ge(sdnt.h, FTS * gg + (2 if tt < 2 else 4))
                                bank = 0 if tt < 2 else 1
                                a = v.tensor_scalar(
                                    out=f_raw[:, tt, gi * 256:(gi + 1) * 256],
                                    in0=ps_f[bank][:, (tt % 2) * 256:(tt % 2) * 256 + 256],
                                    scalar1=1.0, scalar2=None, op0=ALU.mult,
                                    op1=ALU.add, accum_out=s1p4[:, tt, gi:gi + 1])
                                sfev.inc(a)
                        v.wait_ge(sfev.h, FTS * G * (c + 1))
                        for tl in range(FTS):
                            tg = c * FTS + tl
                            a = v.reduce_sum(s1_4[:, tg:tg + 1], s1p4[:, tl, :],
                                             axis=AX.X)
                            fch_v.rt(v, a)
                            a = v.tensor_scalar(out=mean4[:, tg:tg + 1],
                                                in0=s1_4[:, tg:tg + 1],
                                                scalar1=1.0 / C, scalar2=None,
                                                op0=ALU.mult)
                            fch_v.rt(v, a)
                            v.wait_ge(ssq4.h, tg + 1)
                            v.tensor_mul(msq4[:, tg:tg + 1], mean4[:, tg:tg + 1],
                                         mean4[:, tg:tg + 1])
                            a = v.tensor_scalar(out=var4[:, tg:tg + 1],
                                                in0=s2_4[:, tg:tg + 1],
                                                scalar1=1.0 / C, scalar2=EPS,
                                                op0=ALU.mult, op1=ALU.add)
                            fch_v.rt(v, a)
                            a = v.tensor_sub(var4[:, tg:tg + 1], var4[:, tg:tg + 1],
                                             msq4[:, tg:tg + 1])
                            sv4.inc(a)
                        v.wait_ge(sr4.h, c + 1)
                        a = v.scalar_tensor_tensor(
                            out=nmr4[:, c * FTS:(c + 1) * FTS],
                            in0=mean4[:, c * FTS:(c + 1) * FTS], scalar=-1.0,
                            in1=rstd4[:, c * FTS:(c + 1) * FTS],
                            op0=ALU.mult, op1=ALU.mult)
                        sn4.inc(a)


                @blk.tensor
                def _(tn):
                    for c in range(NF):
                        for tl in range(FTS):
                            tg = c * FTS + tl
                            tn.wait_ge(sh2.h, tg + 1)
                            for b in range(4):
                                bg = tg * 4 + b
                                if bg >= 2:
                                    tn.wait_ge(sev3.h, bg - 1)
                                last = None
                                for j in range(8):
                                    ct = 8 * b + j
                                    last = tn.transpose(
                                        ps_t3[bg % 2][:, j, :],
                                        h2[tg % 2][:, ct * 128:(ct + 1) * 128],
                                        identG[:])
                                stp3.inc(last)
                        tn.wait_ge(sev3.h, 4 * FTS * (c + 1))
                        for gi in range(G + 1):
                            gg = c * G + gi
                            if gi < G:
                                tn.wait_ge(swf[gg % 4].h, 32 * (gg // 4 + 1))
                                for q in range(4):
                                    qg = 4 * gg + q
                                    if qg >= 2:
                                        tn.wait_ge(sgel.h, qg - 1)
                                    last = None
                                    for mi in range(2):
                                        mt = 2 * q + mi
                                        for kt_i in range(2):
                                            last = tn.matmul(
                                                ps_m[q % 2][:, mi, :],
                                                up_sb[gi % 4][:, kt_i, mt, :],
                                                h2T[:, 2 * gi + kt_i, :],
                                                start=(kt_i == 0), stop=(kt_i == 1))
                                    sup.inc(last)
                            if gi >= 1:
                                gp = gi - 1
                                ggp = c * G + gp
                                tn.wait_ge(sgel.h, 4 * ggp + 4)
                                for tt in range(FTS):
                                    bank = 0 if tt < 2 else 1
                                    if ggp >= 1:
                                        tn.wait_ge(sfev.h, FTS * (ggp - 1)
                                                   + (2 if tt < 2 else 4))
                                    last = None
                                    for mt in range(8):
                                        last = tn.matmul(
                                            ps_f[bank][:, (tt % 2) * 256:(tt % 2) * 256 + 256],
                                            midT[gp % 2][:, mt, tt * 128:(tt + 1) * 128],
                                            dn_sb[gp % 4][:, mt, :],
                                            start=(mt == 0), stop=(mt == 7))
                                    sdnt.inc(last)

    nc.finalize()
    return nc


def _pack(L0_attn_x, L0_attn_Wqkv, L0_attn_bqkv, L0_attn_Wo, L0_attn_bo,
          L0_ffn_up, L0_ffn_down, L0_ffn_perm, S=2048):
    perm = np.asarray(L0_ffn_perm)
    x = np.asarray(L0_attn_x, dtype=np.float32)[:, :S, :]
    x_p = np.ascontiguousarray(np.take(x, perm, axis=2))
    Wqkv = np.asarray(L0_attn_Wqkv, dtype=np.float32)
    Wqkv_p = np.ascontiguousarray(Wqkv[perm, :])
    bqkv = np.asarray(L0_attn_bqkv, dtype=np.float32)
    Wo = np.asarray(L0_attn_Wo, dtype=np.float32)
    Wo_p = np.ascontiguousarray(Wo[:, perm])
    bo_p = np.asarray(L0_attn_bo, dtype=np.float32)[perm]
    up = np.asarray(L0_ffn_up, dtype=np.float32)
    down = np.asarray(L0_ffn_down, dtype=np.float32)

    def wtile(w):
        return np.ascontiguousarray(
            w.reshape(CT, 128, 8, 128).transpose(2, 1, 0, 3).reshape(8, 128, C))

    shared = {
        "wq": wtile(Wqkv_p[:, 0:1024]),
        "wk": wtile(Wqkv_p[:, 1024:2048]),
        "wv": wtile(Wqkv_p[:, 2048:3072]),
        "bq": np.ascontiguousarray(bqkv[0:1024].reshape(8, 128).T),
        "bk": np.ascontiguousarray(bqkv[1024:2048].reshape(8, 128).T),
        "bv": np.ascontiguousarray(bqkv[2048:3072].reshape(8, 128).T),
        "wo": np.ascontiguousarray(Wo_p.reshape(8, 128, C)),
        "bo": np.ascontiguousarray(np.broadcast_to(bo_p, (128, C))),
        "upw": np.ascontiguousarray(
            up.reshape(G, 2, 128, 8, 128).transpose(0, 2, 1, 3, 4).reshape(G, 128, 2048)),
        "dnw": np.ascontiguousarray(
            down.reshape(G, 8, 128, 256).transpose(0, 2, 1, 3).reshape(G, 128, 2048)),
        "ident": np.eye(128, dtype=ml_dtypes.bfloat16),
    }
    in_maps = []
    for c in range(x_p.shape[0]):
        m = dict(shared)
        m["x"] = np.ascontiguousarray(x_p[c])
        in_maps.append(m)
    return in_maps, perm


def kernel(L0_attn_x, L0_attn_Wqkv, L0_attn_bqkv, L0_attn_Wo, L0_attn_bo,
           L0_ffn_up, L0_ffn_down, L0_ffn_perm):
    S = L0_attn_x.shape[1]
    in_maps, perm = _pack(L0_attn_x, L0_attn_Wqkv, L0_attn_bqkv, L0_attn_Wo,
                          L0_attn_bo, L0_ffn_up, L0_ffn_down, L0_ffn_perm, S=S)
    nc = build_program(S=S)
    res = run_bass_kernel_spmd(nc, in_maps, core_ids=list(range(len(in_maps))))
    out_p = np.stack([np.asarray(r["out"]) for r in res.results], axis=0)
    iperm = np.argsort(perm)
    return np.ascontiguousarray(np.take(out_p, iperm, axis=2)).astype(np.float32)


# version 09 (final)
# speedup vs baseline: 1.0771x; 1.0217x over previous
import numpy as np
import ml_dtypes
from contextlib import ExitStack

import concourse.bass as bass
import concourse.mybir as mybir
from concourse.bass_utils import run_bass_kernel_spmd

F32 = mybir.dt.float32
BF16 = mybir.dt.bfloat16
AF = mybir.ActivationFunctionType
ALU = mybir.AluOpType
AX = mybir.AxisListType

C = 4096
D = 1024
CT = C // 128
G = 16
EPS = 1e-5
ISQ = 1.0 / 32.0


class Sem:
    def __init__(self, ctx, nc, name):
        self.h = ctx.enter_context(nc.semaphore(name))
        self.n = 0

    def inc(self, instr, k=1):
        instr.then_inc(self.h, k)
        self.n += k

    def rt(self, eng, instr):
        instr.then_inc(self.h, 1)
        self.n += 1
        eng.wait_ge(self.h, self.n)


def build_program(S=2048):
    T = S // 128
    QTS = 16 if T % 16 == 0 else T
    NQ = T // QTS
    SC = (QTS * 128) // 512
    FTS = 4
    assert T % FTS == 0
    NF = T // FTS
    AB = (T + 7) // 8

    nc = bass.Bass("TRN2", target_bir_lowering=False)

    x_h = nc.dram_tensor("x", [S, C], F32, kind="ExternalInput").ap()
    wq_h = nc.dram_tensor("wq", [8, 128, C], F32, kind="ExternalInput").ap()
    wk_h = nc.dram_tensor("wk", [8, 128, C], F32, kind="ExternalInput").ap()
    wv_h = nc.dram_tensor("wv", [8, 128, C], F32, kind="ExternalInput").ap()
    bq_h = nc.dram_tensor("bq", [128, 8], F32, kind="ExternalInput").ap()
    bk_h = nc.dram_tensor("bk", [128, 8], F32, kind="ExternalInput").ap()
    bv_h = nc.dram_tensor("bv", [128, 8], F32, kind="ExternalInput").ap()
    wo_h = nc.dram_tensor("wo", [8, 128, C], F32, kind="ExternalInput").ap()
    bo_h = nc.dram_tensor("bo", [128, C], F32, kind="ExternalInput").ap()
    up_h = nc.dram_tensor("upw", [G, 128, 2048], F32, kind="ExternalInput").ap()
    dn_h = nc.dram_tensor("dnw", [G, 128, 2048], F32, kind="ExternalInput").ap()
    id_h = nc.dram_tensor("ident", [128, 128], BF16, kind="ExternalInput").ap()

    out_h = nc.dram_tensor("out", [S, C], F32, kind="ExternalOutput").ap()

    q_hh = nc.dram_tensor("q_scr", [8, 128, S], BF16, kind="Internal").ap()
    k_hh = nc.dram_tensor("k_scr", [8, 128, S], BF16, kind="Internal").ap()
    v_hh = nc.dram_tensor("v_scr", [8, 128, S], BF16, kind="Internal").ap()
    x1_hh = nc.dram_tensor("x1_scr", [T, 128, C], BF16, kind="Internal").ap()
    up_bf = nc.dram_tensor("up_bf", [G, 128, 2048], BF16, kind="Internal").ap()
    dn_bf = nc.dram_tensor("dn_bf", [G, 128, 2048], BF16, kind="Internal").ap()

    with ExitStack() as gctx:
        identG = gctx.enter_context(nc.sbuf_tensor("identG", [128, 128], BF16))

        s_id = Sem(gctx, nc, "s_id")
        sx = [Sem(gctx, nc, f"q_sx{i}") for i in range(3)]
        sbl = Sem(gctx, nc, "q_sbl")
        ssq = Sem(gctx, nc, "q_ssq")
        sv2 = Sem(gctx, nc, "q_sv2")
        srs = Sem(gctx, nc, "q_srs")
        snm = Sem(gctx, nc, "q_snm")
        sh1 = Sem(gctx, nc, "q_sh1")
        stp = Sem(gctx, nc, "q_stp")
        sev = Sem(gctx, nc, "q_sev")
        swl = [Sem(gctx, nc, f"q_swl{i}") for i in range(2)]
        smm = Sem(gctx, nc, "q_smm")
        sqe = Sem(gctx, nc, "q_sqe")
        sqs = [Sem(gctx, nc, f"q_sqs{i}") for i in range(4)]
        skt = Sem(gctx, nc, "a_skt")
        svtg = [Sem(gctx, nc, f"a_svtg{i}") for i in range(4)]
        swo = Sem(gctx, nc, "a_swo")
        sq = [Sem(gctx, nc, f"a_sq{i}") for i in range(2)]
        sxr = [Sem(gctx, nc, f"a_sxr{i}") for i in range(2)]
        s_sc = Sem(gctx, nc, "a_ssc")
        s_mx = Sem(gctx, nc, "a_smx")
        s_ex = Sem(gctx, nc, "a_sex")
        s_at = Sem(gctx, nc, "a_sat")
        s_ae = Sem(gctx, nc, "a_sae")
        s_oa = Sem(gctx, nc, "a_soa")
        s_oe = Sem(gctx, nc, "a_soe")
        s_ot = Sem(gctx, nc, "a_sot")
        s_ote = Sem(gctx, nc, "a_sote")
        s_pj = Sem(gctx, nc, "a_spj")
        s_pje = Sem(gctx, nc, "a_spje")
        s_q2 = Sem(gctx, nc, "a_sq2")
        s_v2 = Sem(gctx, nc, "a_sv2")
        s_r2 = Sem(gctx, nc, "a_sr2")
        s_n2 = Sem(gctx, nc, "a_sn2")
        s_nm = Sem(gctx, nc, "a_snm")
        s_x1 = Sem(gctx, nc, "a_sx1")
        s_x1s = [Sem(gctx, nc, f"a_sx1s{i}") for i in range(2)]
        s_wc = Sem(gctx, nc, "a_swc")
        s_q3 = Sem(gctx, nc, "a_sq3")
        s_v3 = Sem(gctx, nc, "a_sv3")
        s_r3 = Sem(gctx, nc, "a_sr3")
        sxl = Sem(gctx, nc, "f_sxl")
        ssq3 = Sem(gctx, nc, "f_ssq3")
        sv3 = Sem(gctx, nc, "f_sv3")
        sr3 = Sem(gctx, nc, "f_sr3")
        sn3 = Sem(gctx, nc, "f_sn3")
        sh2 = Sem(gctx, nc, "f_sh2")
        stp3 = Sem(gctx, nc, "f_stp3")
        sev3 = Sem(gctx, nc, "f_sev3")
        swf = [Sem(gctx, nc, f"f_swf{i}") for i in range(4)]
        sup = Sem(gctx, nc, "f_sup")
        sgel = Sem(gctx, nc, "f_sgel")
        sdnt = Sem(gctx, nc, "f_sdnt")
        sfev = Sem(gctx, nc, "f_sfev")
        ssq4 = Sem(gctx, nc, "f_ssq4")
        sv4 = Sem(gctx, nc, "f_sv4")
        sr4 = Sem(gctx, nc, "f_sr4")
        sn4 = Sem(gctx, nc, "f_sn4")
        snm4 = Sem(gctx, nc, "f_snm4")
        sadd = Sem(gctx, nc, "f_sadd")
        sost = [Sem(gctx, nc, f"f_sost{i}") for i in range(2)]
        s1_3 = gctx.enter_context(nc.sbuf_tensor("s1_3", [128, T], F32))
        s2_3 = gctx.enter_context(nc.sbuf_tensor("s2_3", [128, T], F32))
        mean3 = gctx.enter_context(nc.sbuf_tensor("mean3", [128, T], F32))
        var3 = gctx.enter_context(nc.sbuf_tensor("var3", [128, T], F32))
        msq3 = gctx.enter_context(nc.sbuf_tensor("msq3", [128, T], F32))
        lnv3 = gctx.enter_context(nc.sbuf_tensor("lnv3", [128, T], F32))
        rstd3 = gctx.enter_context(nc.sbuf_tensor("rstd3", [128, T], F32))
        nmr3 = gctx.enter_context(nc.sbuf_tensor("nmr3", [128, T], F32))
        qch_v = Sem(gctx, nc, "qch_v")
        qch_s = Sem(gctx, nc, "qch_s")
        ach_v = Sem(gctx, nc, "ach_v")
        ach_s = Sem(gctx, nc, "ach_s")
        fch_v = Sem(gctx, nc, "fch_v")
        fch_s = Sem(gctx, nc, "fch_s")

        with ExitStack() as ctx:
            E = ctx.enter_context
            xb = [E(nc.sbuf_tensor(f"xb{i}", [128, C], BF16)) for i in range(3)]
            h1 = [E(nc.sbuf_tensor(f"h1_{i}", [128, C], BF16)) for i in range(2)]
            h1T = E(nc.sbuf_tensor("h1T", [128, CT, QTS * 128], BF16))
            wbuf = [E(nc.sbuf_tensor(f"wbuf{i}", [128, CT, 128], BF16)) for i in range(2)]
            qev = [E(nc.sbuf_tensor(f"qev{i}", [128, 512], BF16)) for i in range(4)]
            bq_s = E(nc.sbuf_tensor("bq_s", [128, 8], F32))
            bk_s = E(nc.sbuf_tensor("bk_s", [128, 8], F32))
            bv_s = E(nc.sbuf_tensor("bv_s", [128, 8], F32))
            s1q = E(nc.sbuf_tensor("s1q", [128, T], F32))
            s2q = E(nc.sbuf_tensor("s2q", [128, T], F32))
            meanq = E(nc.sbuf_tensor("meanq", [128, T], F32))
            varq = E(nc.sbuf_tensor("varq", [128, T], F32))
            msqq = E(nc.sbuf_tensor("msqq", [128, T], F32))
            lnvq = E(nc.sbuf_tensor("lnvq", [128, T], F32))
            rstdq = E(nc.sbuf_tensor("rstdq", [128, T], F32))
            nmrq = E(nc.sbuf_tensor("nmrq", [128, T], F32))
            ps_t = [E(nc.psum_tensor(f"ps_t{i}", [128, 8, 128], BF16)) for i in range(2)]
            ps_q = [E(nc.psum_tensor(f"ps_q{i}", [128, 512], F32)) for i in range(4)]

            with nc.Block() as blk:
                @blk.gpsimd
                def _(g):
                    g.dma_start(out=identG[:], in_=id_h[:]).then_inc(s_id.h, 16)
                    s_id.n += 16
                    for name, dst, src in (("bq", bq_s, bq_h), ("bk", bk_s, bk_h),
                                           ("bv", bv_s, bv_h)):
                        d = g.dma_start(out=dst[:], in_=src[:])
                        sbl.inc(d, 16)
                    for ch in range(NQ):
                        for i in range(QTS):
                            t = ch * QTS + i
                            if t >= 3:
                                g.wait_ge(sh1.h, t - 2)
                                g.wait_ge(sx[t % 3].h, 16 * (t // 3))
                            d = g.dma_start(out=xb[t % 3][:],
                                            in_=x_h[t * 128:(t + 1) * 128, :])
                            sx[t % 3].inc(d, 16)
                        for f in range(24):
                            fg = ch * 24 + f
                            if fg >= 2:
                                g.wait_ge(smm.h, SC * (fg - 1))
                                g.wait_ge(swl[fg % 2].h, 16 * (fg // 2))
                            src = wq_h[f] if f < 8 else (
                                wk_h[f - 8] if f < 16 else wv_h[f - 16])
                            d = g.dma_start(out=wbuf[fg % 2][:], in_=src)
                            swl[fg % 2].inc(d, 16)

                @blk.sync
                def _(sy):
                    for ch in range(NQ):
                        for f in range(24):
                            for s_i in range(SC):
                                e = (ch * 24 + f) * SC + s_i
                                sy.wait_ge(sqe.h, e + 1)
                                if e >= 4:
                                    sy.wait_ge(sqs[e % 4].h, 16 * (e // 4))
                                off = ch * QTS * 128 + s_i * 512
                                if f < 8:
                                    dst = q_hh[f, :, off:off + 512]
                                elif f < 16:
                                    dst = k_hh[f - 8, :, off:off + 512]
                                else:
                                    dst = v_hh[f - 16, :, off:off + 512]
                                d = sy.dma_start(out=dst, in_=qev[e % 4][:])
                                sqs[e % 4].inc(d, 16)
                    for ss in sqs:
                        sy.wait_ge(ss.h, ss.n)

                @blk.scalar
                def _(s):
                    s.wait_ge(sbl.h, 48)
                    for ch in range(NQ):
                        for i in range(QTS):
                            t = ch * QTS + i
                            if t >= 2:
                                s.wait_ge(stp.h, 4 * (t - 1))
                            s.wait_ge(sx[t % 3].h, 16 * (t // 3 + 1))
                            a = s.activation(h1[t % 2][:], xb[t % 3][:], AF.Square,
                                             accum_out=s2q[:, t:t + 1])
                            ssq.inc(a)
                            s.wait_ge(sv2.h, t + 1)
                            a = s.activation(lnvq[:, t:t + 1], varq[:, t:t + 1], AF.Ln)
                            qch_s.rt(s, a)
                            a = s.activation(rstdq[:, t:t + 1], lnvq[:, t:t + 1],
                                             AF.Exp, scale=-0.5)
                            srs.inc(a)
                            s.wait_ge(snm.h, t + 1)
                            a = s.activation(h1[t % 2][:], xb[t % 3][:], AF.Identity,
                                             bias=nmrq[:, t:t + 1],
                                             scale=rstdq[:, t:t + 1])
                            sh1.inc(a)
                        for f in range(24):
                            for s_i in range(SC):
                                grp = (ch * 24 + f) * SC + s_i
                                s.wait_ge(smm.h, grp + 1)
                                if grp >= 4:
                                    s.wait_ge(sqs[grp % 4].h, 16 * (grp // 4))
                                if f < 8:
                                    bias = bq_s[:, f:f + 1]
                                elif f < 16:
                                    bias = bk_s[:, f - 8:f - 7]
                                else:
                                    bias = bv_s[:, f - 16:f - 15]
                                a = s.activation(qev[grp % 4][:], ps_q[grp % 4][:],
                                                 AF.Identity, bias=bias)
                                sqe.inc(a)

                @blk.vector
                def _(v):
                    for t in range(T):
                        v.wait_ge(sx[t % 3].h, 16 * (t // 3 + 1))
                        a = v.reduce_sum(s1q[:, t:t + 1], xb[t % 3][:], axis=AX.X)
                        qch_v.rt(v, a)
                        a = v.tensor_scalar(out=meanq[:, t:t + 1], in0=s1q[:, t:t + 1],
                                            scalar1=1.0 / C, scalar2=None, op0=ALU.mult)
                        qch_v.rt(v, a)
                        v.wait_ge(ssq.h, t + 1)
                        v.tensor_mul(msqq[:, t:t + 1], meanq[:, t:t + 1],
                                     meanq[:, t:t + 1])
                        a = v.tensor_scalar(out=varq[:, t:t + 1], in0=s2q[:, t:t + 1],
                                            scalar1=1.0 / C, scalar2=EPS,
                                            op0=ALU.mult, op1=ALU.add)
                        qch_v.rt(v, a)
                        a = v.tensor_sub(varq[:, t:t + 1], varq[:, t:t + 1],
                                         msqq[:, t:t + 1])
                        sv2.inc(a)
                        v.wait_ge(srs.h, t + 1)
                        a = v.scalar_tensor_tensor(out=nmrq[:, t:t + 1],
                                                   in0=meanq[:, t:t + 1],
                                                   scalar=-1.0, in1=rstdq[:, t:t + 1],
                                                   op0=ALU.mult, op1=ALU.mult)
                        snm.inc(a)
                        i = t % QTS
                        for b in range(4):
                            bg = t * 4 + b
                            v.wait_ge(stp.h, bg + 1)
                            a = v.tensor_copy(
                                h1T[:, 8 * b:8 * b + 8, i * 128:(i + 1) * 128],
                                ps_t[bg % 2][:])
                            sev.inc(a)

                @blk.tensor
                def _(tn):
                    tn.wait_ge(s_id.h, 16)

                    def emit_tile(t):
                        tn.wait_ge(sh1.h, t + 1)
                        for b in range(4):
                            bg = t * 4 + b
                            if bg >= 2:
                                tn.wait_ge(sev.h, bg - 1)
                            last = None
                            for j in range(8):
                                ct = 8 * b + j
                                last = tn.transpose(
                                    ps_t[bg % 2][:, j, :],
                                    h1[t % 2][:, ct * 128:(ct + 1) * 128],
                                    identG[:])
                            stp.inc(last)

                    for ch in range(NQ):
                        for i in range(QTS):
                            emit_tile(ch * QTS + i)
                        for f in range(24):
                            fg = ch * 24 + f
                            tn.wait_ge(swl[fg % 2].h, 16 * (fg // 2 + 1))
                            for s_i in range(SC):
                                grp = fg * SC + s_i
                                need = ch * QTS + min(4 * (s_i + 1), QTS)
                                tn.wait_ge(sev.h, 4 * need)
                                if grp >= 4:
                                    tn.wait_ge(sqe.h, grp - 3)
                                last = None
                                for ct in range(CT):
                                    last = tn.matmul(
                                        ps_q[grp % 4][:],
                                        wbuf[fg % 2][:, ct, :],
                                        h1T[:, ct, s_i * 512:(s_i + 1) * 512],
                                        start=(ct == 0), stop=(ct == CT - 1))
                                smm.inc(last)

        with ExitStack() as ctx:
            E = ctx.enter_context
            kT = E(nc.sbuf_tensor("kT", [128, 8, S], BF16))
            v_sb = E(nc.sbuf_tensor("v_sb", [128, T, D], BF16))
            wo_sb = E(nc.sbuf_tensor("wo_sb", [128, 8, C], BF16))
            bo_sb = E(nc.sbuf_tensor("bo_sb", [128, C], BF16))
            qt = [E(nc.sbuf_tensor(f"qt{i}", [128, 8, 128], BF16)) for i in range(2)]
            probs = E(nc.sbuf_tensor("probs", [128, S], BF16))
            attnT = E(nc.sbuf_tensor("attnT", [128, T, 128], BF16))
            o_sb = E(nc.sbuf_tensor("o_sb", [128, D], BF16))
            oT = [E(nc.sbuf_tensor(f"oT{i}", [128, 8, 128], BF16)) for i in range(2)]
            pb2 = [E(nc.sbuf_tensor(f"pb2_{i}", [128, C], BF16)) for i in range(2)]
            xr = [E(nc.sbuf_tensor(f"xr{i}", [128, C], BF16)) for i in range(2)]
            x1b = [E(nc.sbuf_tensor(f"x1b{i}", [128, C], BF16)) for i in range(2)]
            negmax = E(nc.sbuf_tensor("negmax", [128, T], F32))
            nmax32 = E(nc.sbuf_tensor("nmax32", [128, T], F32))
            sumexp = E(nc.sbuf_tensor("sumexp", [128, T], F32))
            rs_t = E(nc.sbuf_tensor("rs_t", [128, T], F32))
            s1p = E(nc.sbuf_tensor("s1p", [128, 8], F32))
            s1a = E(nc.sbuf_tensor("s1a", [128, T], F32))
            s2a = E(nc.sbuf_tensor("s2a", [128, T], F32))
            meana = E(nc.sbuf_tensor("meana", [128, T], F32))
            vara = E(nc.sbuf_tensor("vara", [128, T], F32))
            msqa = E(nc.sbuf_tensor("msqa", [128, T], F32))
            lnva = E(nc.sbuf_tensor("lnva", [128, T], F32))
            rstda = E(nc.sbuf_tensor("rstda", [128, T], F32))
            nmra = E(nc.sbuf_tensor("nmra", [128, T], F32))
            ps_s = E(nc.psum_tensor("ps_s", [128, S], F32))
            ps_at = E(nc.psum_tensor("ps_at", [128, 8, 128], BF16))
            ps_p = [E(nc.psum_tensor(f"ps_p{i}", [128, 512], F32))
                    for i in range(3)]
            if S >= 2 * D:
                ps_o = ps_s[:, S - D:S]
            else:
                ps_o = E(nc.psum_tensor("ps_o", [128, D], F32))

            with nc.Block() as blk:
                @blk.sync
                def _(sy):
                    for dt in range(8):
                        d = sy.dma_start(out=kT[:, dt, :], in_=k_hh[dt])
                        skt.inc(d, 16)
                    for t in range(2):
                        src = q_hh[:, :, t * 128:(t + 1) * 128].rearrange(
                            "a p m -> p a m")
                        d = sy.dma_start(out=qt[t % 2][:], in_=src)
                        sq[t % 2].inc(d, 16)
                    for kt_i in range(T):
                        src = v_hh[:, :, kt_i * 128:(kt_i + 1) * 128].rearrange(
                            "a p m -> (a p) m")
                        d = sy.dma_start_transpose(v_sb[:, kt_i, :], src)
                        svtg[kt_i // 4].inc(d, 16)
                    for t in range(2, T):
                        sy.wait_ge(s_sc.h, t - 1)
                        sy.wait_ge(sq[t % 2].h, 16 * (t // 2))
                        src = q_hh[:, :, t * 128:(t + 1) * 128].rearrange(
                            "a p m -> p a m")
                        d = sy.dma_start(out=qt[t % 2][:], in_=src)
                        sq[t % 2].inc(d, 16)
                        j = t - 2
                        sy.wait_ge(s_x1.h, j + 1)
                        if j >= 2:
                            sy.wait_ge(s_x1s[j % 2].h, 16 * (j // 2))
                        d = sy.dma_start(out=x1_hh[j], in_=x1b[j % 2][:])
                        s_x1s[j % 2].inc(d, 16)
                    for j in (T - 2, T - 1):
                        sy.wait_ge(s_x1.h, j + 1)
                        sy.wait_ge(s_x1s[j % 2].h, 16 * (j // 2))
                        d = sy.dma_start(out=x1_hh[j], in_=x1b[j % 2][:])
                        s_x1s[j % 2].inc(d, 16)
                    for ss in s_x1s:
                        sy.wait_ge(ss.h, ss.n)

                @blk.gpsimd
                def _(g):
                    for dt in range(8):
                        d = g.dma_start(out=wo_sb[:, dt, :], in_=wo_h[dt])
                        swo.inc(d, 16)
                    d = g.dma_start(out=bo_sb[:], in_=bo_h[:])
                    swo.inc(d, 16)
                    for t in range(T):
                        for j in (2 * t, 2 * t + 1):
                            if j < 2 * G:
                                gi = j // 2
                                if j % 2 == 0:
                                    d = g.dma_start(out=up_bf[gi], in_=up_h[gi])
                                else:
                                    d = g.dma_start(out=dn_bf[gi], in_=dn_h[gi])
                                s_wc.inc(d, 16)
                        if t >= 2:
                            g.wait_ge(s_x1.h, t - 1)
                            g.wait_ge(sxr[t % 2].h, 16 * (t // 2))
                        d = g.dma_start(out=xr[t % 2][:],
                                        in_=x_h[t * 128:(t + 1) * 128, :])
                        sxr[t % 2].inc(d, 16)

                @blk.tensor
                def _(tn):
                    for t in range(T + 1):
                        if t >= 1:
                            tn.wait_ge(s_oe.h, t)
                            tn.wait_ge(s_ae.h, AB * t)
                            if t >= 2:
                                tn.wait_ge(s_ote.h, t - 1)
                            last = None
                            for dt in range(8):
                                last = tn.transpose(
                                    ps_at[:, dt, :],
                                    o_sb[:, dt * 128:(dt + 1) * 128], identG[:])
                            s_ot.inc(last)
                        if t < T:
                            tn.wait_ge(sq[t % 2].h, 16 * (t // 2 + 1))
                            if t == 0:
                                tn.wait_ge(skt.h, skt.n)
                            if t >= 1:
                                tn.wait_ge(s_ex.h, t)
                                if S >= 2 * D:
                                    tn.wait_ge(s_oe.h, t)
                            last = None
                            for s_i in range(S // 512):
                                for dt in range(8):
                                    last = tn.matmul(
                                        ps_s[:, s_i * 512:(s_i + 1) * 512],
                                        qt[t % 2][:, dt, :],
                                        kT[:, dt, s_i * 512:(s_i + 1) * 512],
                                        start=(dt == 0), stop=(dt == 7))
                            s_sc.inc(last)
                        if t >= 1:
                            tn.wait_ge(s_ote.h, t)
                            if t == 1:
                                tn.wait_ge(swo.h, swo.n)
                            for cq in range(8):
                                pg = 8 * (t - 1) + cq
                                if pg >= 3:
                                    tn.wait_ge(s_pje.h, pg - 2)
                                last = None
                                for dt in range(8):
                                    last = tn.matmul(
                                        ps_p[pg % 3][:],
                                        oT[(t - 1) % 2][:, dt, :],
                                        wo_sb[:, dt, cq * 512:(cq + 1) * 512],
                                        start=(dt == 0), stop=(dt == 7))
                                s_pj.inc(last)
                        if t < T:
                            tn.wait_ge(s_ex.h, t + 1)
                            for b in range(AB):
                                bg = AB * t + b
                                if bg >= 1:
                                    tn.wait_ge(s_ae.h, bg)
                                nb = min(8, T - 8 * b)
                                last = None
                                for j in range(nb):
                                    kt_i = 8 * b + j
                                    last = tn.transpose(
                                        ps_at[:, j, :],
                                        probs[:, kt_i * 128:(kt_i + 1) * 128],
                                        identG[:])
                                s_at.inc(last)
                            if t >= 1:
                                tn.wait_ge(s_oe.h, t)
                            last1 = None
                            for kt_i in range(T):
                                if t == 0 and kt_i % 4 == 0:
                                    tn.wait_ge(svtg[kt_i // 4].h, 64)
                                if kt_i % 8 == 0:
                                    tn.wait_ge(s_ae.h, AB * t + kt_i // 8 + 1)
                                src = attnT[:, kt_i, :]
                                tn.matmul(ps_o[:, 0:512], src,
                                          v_sb[:, kt_i, 0:512],
                                          start=(kt_i == 0), stop=(kt_i == T - 1),
                                          skip_group_check=True)
                                last1 = tn.matmul(ps_o[:, 512:1024], src,
                                                  v_sb[:, kt_i, 512:1024],
                                                  start=(kt_i == 0),
                                                  stop=(kt_i == T - 1),
                                                  skip_group_check=True)
                            s_oa.inc(last1)

                @blk.vector
                def _(v):
                    for t in range(T + 1):
                        if t >= 1:
                            v.wait_ge(s_ot.h, t)
                            if t >= 3:
                                v.wait_ge(s_pj.h, 8 * (t - 2))
                            a = v.tensor_copy(oT[(t - 1) % 2][:], ps_at[:])
                            s_ote.inc(a)
                        if t < T:
                            v.wait_ge(s_sc.h, t + 1)
                            a = v.reduce_max(negmax[:, t:t + 1], ps_s[:], axis=AX.X,
                                             negate=True)
                            ach_v.rt(v, a)
                            a = v.tensor_scalar(out=nmax32[:, t:t + 1],
                                                in0=negmax[:, t:t + 1],
                                                scalar1=ISQ, scalar2=None,
                                                op0=ALU.mult)
                            s_mx.inc(a)
                        if t >= 1:
                            for cq in range(8):
                                pg = 8 * (t - 1) + cq
                                v.wait_ge(s_pj.h, pg + 1)
                                if cq == 0 and t >= 3:
                                    v.wait_ge(s_nm.h, t - 2)
                                a = v.scalar_tensor_tensor(
                                    out=pb2[(t - 1) % 2][:, cq * 512:(cq + 1) * 512],
                                    in0=ps_p[pg % 3][:], scalar=rs_t[:, t - 1:t],
                                    in1=bo_sb[:, cq * 512:(cq + 1) * 512],
                                    op0=ALU.mult, op1=ALU.add,
                                    accum_out=s1p[:, cq:cq + 1])
                                s_pje.inc(a)
                        if t < T:
                            for b in range(AB):
                                v.wait_ge(s_at.h, AB * t + b + 1)
                                if t >= 1 and b == 0:
                                    v.wait_ge(s_oa.h, t)
                                nb = min(8, T - 8 * b)
                                a = v.tensor_copy(attnT[:, 8 * b:8 * b + nb, :],
                                                  ps_at[:, 0:nb, :])
                                s_ae.inc(a)
                            v.wait_ge(s_ex.h, t + 1)
                            a = v.reciprocal(rs_t[:, t:t + 1], sumexp[:, t:t + 1])
                            ach_v.rt(v, a)
                        if t >= 1:
                            v.wait_ge(s_pje.h, 8 * t)
                            a = v.reduce_sum(s1a[:, t - 1:t], s1p[:], axis=AX.X)
                            ach_v.rt(v, a)
                            a = v.tensor_scalar(out=meana[:, t - 1:t],
                                                in0=s1a[:, t - 1:t],
                                                scalar1=1.0 / C, scalar2=None,
                                                op0=ALU.mult)
                            ach_v.rt(v, a)
                            v.wait_ge(s_q2.h, t)
                            v.tensor_mul(msqa[:, t - 1:t], meana[:, t - 1:t],
                                         meana[:, t - 1:t])
                            a = v.tensor_scalar(out=vara[:, t - 1:t],
                                                in0=s2a[:, t - 1:t],
                                                scalar1=1.0 / C, scalar2=EPS,
                                                op0=ALU.mult, op1=ALU.add)
                            ach_v.rt(v, a)
                            a = v.tensor_sub(vara[:, t - 1:t], vara[:, t - 1:t],
                                             msqa[:, t - 1:t])
                            s_v2.inc(a)
                        if t >= 1:
                            v.wait_ge(s_r2.h, t)
                            a = v.scalar_tensor_tensor(
                                out=nmra[:, t - 1:t], in0=meana[:, t - 1:t],
                                scalar=-1.0, in1=rstda[:, t - 1:t],
                                op0=ALU.mult, op1=ALU.mult)
                            s_n2.inc(a)
                            v.wait_ge(s_nm.h, t)
                            v.wait_ge(sxr[(t - 1) % 2].h, 16 * ((t - 1) // 2 + 1))
                            a = v.tensor_add(x1b[(t - 1) % 2][:],
                                             x1b[(t - 1) % 2][:],
                                             xr[(t - 1) % 2][:])
                            s_x1.inc(a)
                            v.wait_ge(s_x1.h, t)
                            a = v.reduce_sum(s1_3[:, t - 1:t],
                                             x1b[(t - 1) % 2][:], axis=AX.X)
                            ach_v.rt(v, a)
                            a = v.tensor_scalar(out=mean3[:, t - 1:t],
                                                in0=s1_3[:, t - 1:t],
                                                scalar1=1.0 / C, scalar2=None,
                                                op0=ALU.mult)
                            ach_v.rt(v, a)
                            v.wait_ge(s_q3.h, t)
                            v.tensor_mul(msq3[:, t - 1:t], mean3[:, t - 1:t],
                                         mean3[:, t - 1:t])
                            a = v.tensor_scalar(out=var3[:, t - 1:t],
                                                in0=s2_3[:, t - 1:t],
                                                scalar1=1.0 / C, scalar2=EPS,
                                                op0=ALU.mult, op1=ALU.add)
                            ach_v.rt(v, a)
                            a = v.tensor_sub(var3[:, t - 1:t], var3[:, t - 1:t],
                                             msq3[:, t - 1:t])
                            s_v3.inc(a)
                            v.wait_ge(s_r3.h, t)
                            a = v.scalar_tensor_tensor(
                                out=nmr3[:, t - 1:t], in0=mean3[:, t - 1:t],
                                scalar=-1.0, in1=rstd3[:, t - 1:t],
                                op0=ALU.mult, op1=ALU.mult)
                            ach_v.rt(v, a)

                @blk.scalar
                def _(s):
                    for t in range(T + 1):
                        if t < T:
                            s.wait_ge(s_mx.h, t + 1)
                            if t >= 1:
                                s.wait_ge(s_at.h, AB * t)
                            a = s.activation(probs[:], ps_s[:], AF.Exp,
                                             bias=nmax32[:, t:t + 1], scale=ISQ,
                                             accum_out=sumexp[:, t:t + 1])
                            s_ex.inc(a)
                        if t < T:
                            s.wait_ge(s_oa.h, t + 1)
                            s.wait_ge(s_ot.h, t)
                            a = s.activation(o_sb[:], ps_o[:], AF.Identity)
                            s_oe.inc(a)
                        if t >= 1:
                            s.wait_ge(s_pje.h, 8 * t)
                            if t >= 3:
                                s.wait_ge(s_x1s[(t - 1) % 2].h,
                                          16 * ((t - 3) // 2 + 1))
                            a = s.activation(x1b[(t - 1) % 2][:],
                                             pb2[(t - 1) % 2][:],
                                             AF.Square, accum_out=s2a[:, t - 1:t])
                            s_q2.inc(a)
                        if t >= 1:
                            s.wait_ge(s_v2.h, t)
                            a = s.activation(lnva[:, t - 1:t], vara[:, t - 1:t], AF.Ln)
                            ach_s.rt(s, a)
                            a = s.activation(rstda[:, t - 1:t], lnva[:, t - 1:t],
                                             AF.Exp, scale=-0.5)
                            s_r2.inc(a)
                            s.wait_ge(s_n2.h, t)
                            a = s.activation(x1b[(t - 1) % 2][:],
                                             pb2[(t - 1) % 2][:], AF.Identity,
                                             bias=nmra[:, t - 1:t],
                                             scale=rstda[:, t - 1:t])
                            s_nm.inc(a)
                            s.wait_ge(s_x1.h, t)
                            a = s.activation(pb2[(t - 1) % 2][:],
                                             x1b[(t - 1) % 2][:], AF.Square,
                                             accum_out=s2_3[:, t - 1:t])
                            s_q3.inc(a)
                            s.wait_ge(s_v3.h, t)
                            a = s.activation(lnv3[:, t - 1:t], var3[:, t - 1:t],
                                             AF.Ln)
                            ach_s.rt(s, a)
                            a = s.activation(rstd3[:, t - 1:t], lnv3[:, t - 1:t],
                                             AF.Exp, scale=-0.5)
                            s_r3.inc(a)

        with ExitStack() as ctx:
            E = ctx.enter_context
            x1c = E(nc.sbuf_tensor("x1c", [128, FTS, C], BF16))
            h2 = [E(nc.sbuf_tensor(f"h2_{i}", [128, C], BF16)) for i in range(2)]
            h2T = E(nc.sbuf_tensor("h2T", [128, CT, FTS * 128], BF16))
            up_sb = [E(nc.sbuf_tensor(f"up_sb{i}", [128, 2, 8, 128], BF16))
                     for i in range(4)]
            dn_sb = [E(nc.sbuf_tensor(f"dn_sb{i}", [128, 8, 256], BF16))
                     for i in range(4)]
            midT = [E(nc.sbuf_tensor(f"midT{i}", [128, 8, FTS * 128], BF16))
                    for i in range(2)]
            f_raw = E(nc.sbuf_tensor("f_raw", [128, FTS, C], BF16))
            scr3 = E(nc.sbuf_tensor("scr3", [128, C], BF16))
            outb = [E(nc.sbuf_tensor(f"outb{i}", [128, C], F32)) for i in range(2)]
            s1p4 = E(nc.sbuf_tensor("s1p4", [128, FTS, G], F32))
            s1_4 = E(nc.sbuf_tensor("s1_4", [128, T], F32))
            s2_4 = E(nc.sbuf_tensor("s2_4", [128, T], F32))
            mean4 = E(nc.sbuf_tensor("mean4", [128, T], F32))
            var4 = E(nc.sbuf_tensor("var4", [128, T], F32))
            msq4 = E(nc.sbuf_tensor("msq4", [128, T], F32))
            lnv4 = E(nc.sbuf_tensor("lnv4", [128, T], F32))
            rstd4 = E(nc.sbuf_tensor("rstd4", [128, T], F32))
            nmr4 = E(nc.sbuf_tensor("nmr4", [128, T], F32))
            ps_m = [E(nc.psum_tensor(f"ps_m{i}", [128, 2, 512], F32))
                    for i in range(2)]
            ps_f = [E(nc.psum_tensor(f"ps_f{i}", [128, 512], F32)) for i in range(2)]
            ps_t3 = [E(nc.psum_tensor(f"ps_t3{i}", [128, 8, 128], BF16))
                     for i in range(2)]

            with nc.Block() as blk:
                @blk.sync
                def _(sy):
                    for c in range(NF):
                        if c >= 1:
                            sy.wait_ge(sadd.h, FTS * c)
                            sy.wait_ge(sxl.h, 16 * c)
                        src = x1_hh[c * FTS:(c + 1) * FTS].rearrange(
                            "a p m -> p a m")
                        d = sy.dma_start(out=x1c[:], in_=src)
                        sxl.inc(d, 16)
                        for tl in range(FTS):
                            tg = c * FTS + tl
                            sy.wait_ge(sadd.h, tg + 1)
                            if tg >= 2:
                                sy.wait_ge(sost[tg % 2].h, 16 * (tg // 2))
                            d = sy.dma_start(out=out_h[tg * 128:(tg + 1) * 128, :],
                                             in_=outb[tl % 2][:])
                            sost[tg % 2].inc(d, 16)
                    for ss in sost:
                        sy.wait_ge(ss.h, ss.n)

                @blk.gpsimd
                def _(g):
                    g.wait_ge(s_wc.h, 32 * 16)
                    for c in range(NF):
                        for gi in range(G):
                            gg = c * G + gi
                            if gg >= 4:
                                g.wait_ge(sdnt.h, FTS * (gg - 3))
                                g.wait_ge(swf[gg % 4].h, 32 * (gg // 4))
                            d = g.dma_start(out=up_sb[gi % 4][:], in_=up_bf[gi])
                            swf[gg % 4].inc(d, 16)
                            d = g.dma_start(out=dn_sb[gi % 4][:], in_=dn_bf[gi])
                            swf[gg % 4].inc(d, 16)

                @blk.scalar
                def _(s):
                    for c in range(NF):
                        s.wait_ge(sxl.h, 16 * (c + 1))
                        for tl in range(FTS):
                            tg = c * FTS + tl
                            if tg >= 2:
                                s.wait_ge(stp3.h, 4 * (tg - 1))
                            a = s.activation(h2[tg % 2][:], x1c[:, tl, :], AF.Identity,
                                             bias=nmr3[:, tg:tg + 1],
                                             scale=rstd3[:, tg:tg + 1])
                            sh2.inc(a)
                        for gi in range(G):
                            gg = c * G + gi
                            for q in range(4):
                                s.wait_ge(sup.h, 4 * gg + q + 1)
                                if gg >= 2 and q == 0:
                                    s.wait_ge(sdnt.h, FTS * (gg - 1))
                                a = s.activation(midT[gi % 2][:, 2 * q:2 * q + 2, :],
                                                 ps_m[q % 2][:], AF.Gelu)
                                sgel.inc(a)
                        s.wait_ge(sfev.h, FTS * G * (c + 1))
                        for tl in range(FTS):
                            tg = c * FTS + tl
                            if tg >= 1:
                                s.wait_ge(ssq4.h, tg)
                            a = s.activation(scr3[:], f_raw[:, tl, :],
                                             AF.Square,
                                             accum_out=s2_4[:, tg:tg + 1])
                            ssq4.inc(a)
                        s.wait_ge(sv4.h, FTS * (c + 1))
                        a = s.activation(lnv4[:, c * FTS:(c + 1) * FTS],
                                         var4[:, c * FTS:(c + 1) * FTS], AF.Ln)
                        fch_s.rt(s, a)
                        a = s.activation(rstd4[:, c * FTS:(c + 1) * FTS],
                                         lnv4[:, c * FTS:(c + 1) * FTS],
                                         AF.Exp, scale=-0.5)
                        sr4.inc(a)
                        s.wait_ge(sn4.h, c + 1)
                        for tl in range(FTS):
                            tg = c * FTS + tl
                            if tg >= 2:
                                s.wait_ge(sost[tg % 2].h, 16 * (tg // 2))
                            a = s.activation(outb[tl % 2][:], f_raw[:, tl, :],
                                             AF.Identity, bias=nmr4[:, tg:tg + 1],
                                             scale=rstd4[:, tg:tg + 1])
                            snm4.inc(a)

                @blk.vector
                def _(v):
                    for c in range(NF):
                        for tl in range(FTS):
                            tg = c * FTS + tl
                            for b in range(4):
                                bg = tg * 4 + b
                                v.wait_ge(stp3.h, bg + 1)
                                a = v.tensor_copy(
                                    h2T[:, 8 * b:8 * b + 8, tl * 128:(tl + 1) * 128],
                                    ps_t3[bg % 2][:])
                                sev3.inc(a)
                        if c >= 1:
                            v.wait_ge(snm4.h, FTS * c)
                        for gi in range(G):
                            gg = c * G + gi
                            for tt in range(FTS):
                                v.wait_ge(sdnt.h, FTS * gg + (2 if tt < 2 else 4))
                                bank = 0 if tt < 2 else 1
                                a = v.tensor_scalar(
                                    out=f_raw[:, tt, gi * 256:(gi + 1) * 256],
                                    in0=ps_f[bank][:, (tt % 2) * 256:(tt % 2) * 256 + 256],
                                    scalar1=1.0, scalar2=None, op0=ALU.mult,
                                    op1=ALU.add, accum_out=s1p4[:, tt, gi:gi + 1])
                                sfev.inc(a)
                        v.wait_ge(sfev.h, FTS * G * (c + 1))
                        for tl in range(FTS):
                            tg = c * FTS + tl
                            a = v.reduce_sum(s1_4[:, tg:tg + 1], s1p4[:, tl, :],
                                             axis=AX.X)
                            fch_v.rt(v, a)
                            a = v.tensor_scalar(out=mean4[:, tg:tg + 1],
                                                in0=s1_4[:, tg:tg + 1],
                                                scalar1=1.0 / C, scalar2=None,
                                                op0=ALU.mult)
                            fch_v.rt(v, a)
                            v.wait_ge(ssq4.h, tg + 1)
                            v.tensor_mul(msq4[:, tg:tg + 1], mean4[:, tg:tg + 1],
                                         mean4[:, tg:tg + 1])
                            a = v.tensor_scalar(out=var4[:, tg:tg + 1],
                                                in0=s2_4[:, tg:tg + 1],
                                                scalar1=1.0 / C, scalar2=EPS,
                                                op0=ALU.mult, op1=ALU.add)
                            fch_v.rt(v, a)
                            a = v.tensor_sub(var4[:, tg:tg + 1], var4[:, tg:tg + 1],
                                             msq4[:, tg:tg + 1])
                            sv4.inc(a)
                        v.wait_ge(sr4.h, c + 1)
                        a = v.scalar_tensor_tensor(
                            out=nmr4[:, c * FTS:(c + 1) * FTS],
                            in0=mean4[:, c * FTS:(c + 1) * FTS], scalar=-1.0,
                            in1=rstd4[:, c * FTS:(c + 1) * FTS],
                            op0=ALU.mult, op1=ALU.mult)
                        sn4.inc(a)
                        for tl in range(FTS):
                            tg = c * FTS + tl
                            v.wait_ge(snm4.h, tg + 1)
                            a = v.tensor_add(outb[tl % 2][:], outb[tl % 2][:],
                                             x1c[:, tl, :])
                            sadd.inc(a)

                @blk.tensor
                def _(tn):
                    for c in range(NF):
                        for tl in range(FTS):
                            tg = c * FTS + tl
                            tn.wait_ge(sh2.h, tg + 1)
                            for b in range(4):
                                bg = tg * 4 + b
                                if bg >= 2:
                                    tn.wait_ge(sev3.h, bg - 1)
                                last = None
                                for j in range(8):
                                    ct = 8 * b + j
                                    last = tn.transpose(
                                        ps_t3[bg % 2][:, j, :],
                                        h2[tg % 2][:, ct * 128:(ct + 1) * 128],
                                        identG[:])
                                stp3.inc(last)
                        tn.wait_ge(sev3.h, 4 * FTS * (c + 1))
                        for gi in range(G + 1):
                            gg = c * G + gi
                            if gi < G:
                                tn.wait_ge(swf[gg % 4].h, 32 * (gg // 4 + 1))
                                for q in range(4):
                                    qg = 4 * gg + q
                                    if qg >= 2:
                                        tn.wait_ge(sgel.h, qg - 1)
                                    last = None
                                    for mi in range(2):
                                        mt = 2 * q + mi
                                        for kt_i in range(2):
                                            last = tn.matmul(
                                                ps_m[q % 2][:, mi, :],
                                                up_sb[gi % 4][:, kt_i, mt, :],
                                                h2T[:, 2 * gi + kt_i, :],
                                                start=(kt_i == 0), stop=(kt_i == 1))
                                    sup.inc(last)
                            if gi >= 1:
                                gp = gi - 1
                                ggp = c * G + gp
                                tn.wait_ge(sgel.h, 4 * ggp + 4)
                                for tt in range(FTS):
                                    bank = 0 if tt < 2 else 1
                                    if ggp >= 1:
                                        tn.wait_ge(sfev.h, FTS * (ggp - 1)
                                                   + (2 if tt < 2 else 4))
                                    last = None
                                    for mt in range(8):
                                        last = tn.matmul(
                                            ps_f[bank][:, (tt % 2) * 256:(tt % 2) * 256 + 256],
                                            midT[gp % 2][:, mt, tt * 128:(tt + 1) * 128],
                                            dn_sb[gp % 4][:, mt, :],
                                            start=(mt == 0), stop=(mt == 7))
                                    sdnt.inc(last)

    nc.finalize()
    return nc


def _pack(L0_attn_x, L0_attn_Wqkv, L0_attn_bqkv, L0_attn_Wo, L0_attn_bo,
          L0_ffn_up, L0_ffn_down, L0_ffn_perm, S=2048):
    perm = np.asarray(L0_ffn_perm)
    x = np.asarray(L0_attn_x, dtype=np.float32)[:, :S, :]
    x_p = np.ascontiguousarray(np.take(x, perm, axis=2))
    Wqkv = np.asarray(L0_attn_Wqkv, dtype=np.float32)
    Wqkv_p = np.ascontiguousarray(Wqkv[perm, :])
    bqkv = np.asarray(L0_attn_bqkv, dtype=np.float32)
    Wo = np.asarray(L0_attn_Wo, dtype=np.float32)
    Wo_p = np.ascontiguousarray(Wo[:, perm])
    bo_p = np.asarray(L0_attn_bo, dtype=np.float32)[perm]
    up = np.asarray(L0_ffn_up, dtype=np.float32)
    down = np.asarray(L0_ffn_down, dtype=np.float32)

    def wtile(w):
        return np.ascontiguousarray(
            w.reshape(CT, 128, 8, 128).transpose(2, 1, 0, 3).reshape(8, 128, C))

    shared = {
        "wq": wtile(Wqkv_p[:, 0:1024]),
        "wk": wtile(Wqkv_p[:, 1024:2048]),
        "wv": wtile(Wqkv_p[:, 2048:3072]),
        "bq": np.ascontiguousarray(bqkv[0:1024].reshape(8, 128).T),
        "bk": np.ascontiguousarray(bqkv[1024:2048].reshape(8, 128).T),
        "bv": np.ascontiguousarray(bqkv[2048:3072].reshape(8, 128).T),
        "wo": np.ascontiguousarray(Wo_p.reshape(8, 128, C)),
        "bo": np.ascontiguousarray(np.broadcast_to(bo_p, (128, C))),
        "upw": np.ascontiguousarray(
            up.reshape(G, 2, 128, 8, 128).transpose(0, 2, 1, 3, 4).reshape(G, 128, 2048)),
        "dnw": np.ascontiguousarray(
            down.reshape(G, 8, 128, 256).transpose(0, 2, 1, 3).reshape(G, 128, 2048)),
        "ident": np.eye(128, dtype=ml_dtypes.bfloat16),
    }
    in_maps = []
    for c in range(x_p.shape[0]):
        m = dict(shared)
        m["x"] = np.ascontiguousarray(x_p[c])
        in_maps.append(m)
    return in_maps, perm


def kernel(L0_attn_x, L0_attn_Wqkv, L0_attn_bqkv, L0_attn_Wo, L0_attn_bo,
           L0_ffn_up, L0_ffn_down, L0_ffn_perm):
    S = L0_attn_x.shape[1]
    in_maps, perm = _pack(L0_attn_x, L0_attn_Wqkv, L0_attn_bqkv, L0_attn_Wo,
                          L0_attn_bo, L0_ffn_up, L0_ffn_down, L0_ffn_perm, S=S)
    nc = build_program(S=S)
    res = run_bass_kernel_spmd(nc, in_maps, core_ids=list(range(len(in_maps))))
    out_p = np.stack([np.asarray(r["out"]) for r in res.results], axis=0)
    iperm = np.argsort(perm)
    return np.ascontiguousarray(np.take(out_p, iperm, axis=2)).astype(np.float32)


# version 10
# speedup vs baseline: 1.0923x; 1.0141x over previous
import numpy as np
import ml_dtypes
from contextlib import ExitStack

import concourse.bass as bass
import concourse.mybir as mybir
from concourse.bass_utils import run_bass_kernel_spmd

F32 = mybir.dt.float32
BF16 = mybir.dt.bfloat16
AF = mybir.ActivationFunctionType
ALU = mybir.AluOpType
AX = mybir.AxisListType

C = 4096
D = 1024
CT = C // 128
G = 16
EPS = 1e-5
ISQ = 1.0 / 32.0


class Sem:
    def __init__(self, ctx, nc, name):
        self.h = ctx.enter_context(nc.semaphore(name))
        self.n = 0

    def inc(self, instr, k=1):
        instr.then_inc(self.h, k)
        self.n += k

    def rt(self, eng, instr):
        instr.then_inc(self.h, 1)
        self.n += 1
        eng.wait_ge(self.h, self.n)


def build_program(S=2048):
    T = S // 128
    QTS = 16 if T % 16 == 0 else T
    NQ = T // QTS
    SC = (QTS * 128) // 512
    FTS = 4
    assert T % FTS == 0
    NF = T // FTS
    AB = (T + 7) // 8

    nc = bass.Bass("TRN2", target_bir_lowering=False)

    x_h = nc.dram_tensor("x", [S, C], F32, kind="ExternalInput").ap()
    wq_h = nc.dram_tensor("wq", [8, 128, C], F32, kind="ExternalInput").ap()
    wk_h = nc.dram_tensor("wk", [8, 128, C], F32, kind="ExternalInput").ap()
    wv_h = nc.dram_tensor("wv", [8, 128, C], F32, kind="ExternalInput").ap()
    bq_h = nc.dram_tensor("bq", [128, 8], F32, kind="ExternalInput").ap()
    bk_h = nc.dram_tensor("bk", [128, 8], F32, kind="ExternalInput").ap()
    bv_h = nc.dram_tensor("bv", [128, 8], F32, kind="ExternalInput").ap()
    wo_h = nc.dram_tensor("wo", [8, 128, C], F32, kind="ExternalInput").ap()
    bo_h = nc.dram_tensor("bo", [128, C], F32, kind="ExternalInput").ap()
    up_h = nc.dram_tensor("upw", [G, 128, 2048], F32, kind="ExternalInput").ap()
    dn_h = nc.dram_tensor("dnw", [G, 128, 2048], F32, kind="ExternalInput").ap()
    id_h = nc.dram_tensor("ident", [128, 128], BF16, kind="ExternalInput").ap()

    out_h = nc.dram_tensor("out", [S, C], F32, kind="ExternalOutput").ap()

    q_hh = nc.dram_tensor("q_scr", [8, 128, S], BF16, kind="Internal").ap()
    k_hh = nc.dram_tensor("k_scr", [8, 128, S], BF16, kind="Internal").ap()
    v_hh = nc.dram_tensor("v_scr", [8, 128, S], BF16, kind="Internal").ap()
    x1_hh = nc.dram_tensor("x1_scr", [T, 128, C], BF16, kind="Internal").ap()
    up_bf = nc.dram_tensor("up_bf", [G, 128, 2048], BF16, kind="Internal").ap()
    dn_bf = nc.dram_tensor("dn_bf", [G, 128, 2048], BF16, kind="Internal").ap()

    with ExitStack() as gctx:
        identG = gctx.enter_context(nc.sbuf_tensor("identG", [128, 128], BF16))

        s_id = Sem(gctx, nc, "s_id")
        sx = [Sem(gctx, nc, f"q_sx{i}") for i in range(3)]
        sbl = Sem(gctx, nc, "q_sbl")
        ssq = Sem(gctx, nc, "q_ssq")
        sv2 = Sem(gctx, nc, "q_sv2")
        srs = Sem(gctx, nc, "q_srs")
        snm = Sem(gctx, nc, "q_snm")
        sh1 = Sem(gctx, nc, "q_sh1")
        stp = Sem(gctx, nc, "q_stp")
        sev = Sem(gctx, nc, "q_sev")
        swl = [Sem(gctx, nc, f"q_swl{i}") for i in range(2)]
        smm = Sem(gctx, nc, "q_smm")
        sqe = Sem(gctx, nc, "q_sqe")
        sqs = [Sem(gctx, nc, f"q_sqs{i}") for i in range(4)]
        skt = Sem(gctx, nc, "a_skt")
        svtg = [Sem(gctx, nc, f"a_svtg{i}") for i in range(4)]
        swo = Sem(gctx, nc, "a_swo")
        sq = [Sem(gctx, nc, f"a_sq{i}") for i in range(2)]
        sxr = [Sem(gctx, nc, f"a_sxr{i}") for i in range(2)]
        s_sc = Sem(gctx, nc, "a_ssc")
        s_mx = Sem(gctx, nc, "a_smx")
        s_ex = Sem(gctx, nc, "a_sex")
        s_at = Sem(gctx, nc, "a_sat")
        s_ae = Sem(gctx, nc, "a_sae")
        s_oa = Sem(gctx, nc, "a_soa")
        s_oe = Sem(gctx, nc, "a_soe")
        s_ot = Sem(gctx, nc, "a_sot")
        s_ote = Sem(gctx, nc, "a_sote")
        s_pj = Sem(gctx, nc, "a_spj")
        s_pje = Sem(gctx, nc, "a_spje")
        s_q2 = Sem(gctx, nc, "a_sq2")
        s_v2 = Sem(gctx, nc, "a_sv2")
        s_r2 = Sem(gctx, nc, "a_sr2")
        s_n2 = Sem(gctx, nc, "a_sn2")
        s_nm = Sem(gctx, nc, "a_snm")
        s_x1 = Sem(gctx, nc, "a_sx1")
        s_x1s = [Sem(gctx, nc, f"a_sx1s{i}") for i in range(2)]
        s_wc = Sem(gctx, nc, "a_swc")
        s_q3 = Sem(gctx, nc, "a_sq3")
        s_v3 = Sem(gctx, nc, "a_sv3")
        s_r3 = Sem(gctx, nc, "a_sr3")
        sxl = Sem(gctx, nc, "f_sxl")
        ssq3 = Sem(gctx, nc, "f_ssq3")
        sv3 = Sem(gctx, nc, "f_sv3")
        sr3 = Sem(gctx, nc, "f_sr3")
        sn3 = Sem(gctx, nc, "f_sn3")
        sh2 = Sem(gctx, nc, "f_sh2")
        stp3 = Sem(gctx, nc, "f_stp3")
        sev3 = Sem(gctx, nc, "f_sev3")
        swf = [Sem(gctx, nc, f"f_swf{i}") for i in range(4)]
        sup = Sem(gctx, nc, "f_sup")
        sgel = Sem(gctx, nc, "f_sgel")
        sdnt = Sem(gctx, nc, "f_sdnt")
        sfev = Sem(gctx, nc, "f_sfev")
        ssq4 = Sem(gctx, nc, "f_ssq4")
        sv4 = Sem(gctx, nc, "f_sv4")
        sr4 = Sem(gctx, nc, "f_sr4")
        sn4 = Sem(gctx, nc, "f_sn4")
        snm4 = Sem(gctx, nc, "f_snm4")
        sadd = Sem(gctx, nc, "f_sadd")
        sost = [Sem(gctx, nc, f"f_sost{i}") for i in range(2)]
        s1_3 = gctx.enter_context(nc.sbuf_tensor("s1_3", [128, T], F32))
        s2_3 = gctx.enter_context(nc.sbuf_tensor("s2_3", [128, T], F32))
        mean3 = gctx.enter_context(nc.sbuf_tensor("mean3", [128, T], F32))
        var3 = gctx.enter_context(nc.sbuf_tensor("var3", [128, T], F32))
        msq3 = gctx.enter_context(nc.sbuf_tensor("msq3", [128, T], F32))
        lnv3 = gctx.enter_context(nc.sbuf_tensor("lnv3", [128, T], F32))
        rstd3 = gctx.enter_context(nc.sbuf_tensor("rstd3", [128, T], F32))
        nmr3 = gctx.enter_context(nc.sbuf_tensor("nmr3", [128, T], F32))
        qch_v = Sem(gctx, nc, "qch_v")
        qch_s = Sem(gctx, nc, "qch_s")
        ach_v = Sem(gctx, nc, "ach_v")
        ach_s = Sem(gctx, nc, "ach_s")
        fch_v = Sem(gctx, nc, "fch_v")
        fch_s = Sem(gctx, nc, "fch_s")

        with ExitStack() as ctx:
            E = ctx.enter_context
            xb = [E(nc.sbuf_tensor(f"xb{i}", [128, C], BF16)) for i in range(3)]
            h1 = [E(nc.sbuf_tensor(f"h1_{i}", [128, C], BF16)) for i in range(2)]
            h1T = E(nc.sbuf_tensor("h1T", [128, CT, QTS * 128], BF16))
            wbuf = [E(nc.sbuf_tensor(f"wbuf{i}", [128, CT, 128], BF16)) for i in range(2)]
            qev = [E(nc.sbuf_tensor(f"qev{i}", [128, 512], BF16)) for i in range(4)]
            bq_s = E(nc.sbuf_tensor("bq_s", [128, 8], F32))
            bk_s = E(nc.sbuf_tensor("bk_s", [128, 8], F32))
            bv_s = E(nc.sbuf_tensor("bv_s", [128, 8], F32))
            s1q = E(nc.sbuf_tensor("s1q", [128, T], F32))
            s2q = E(nc.sbuf_tensor("s2q", [128, T], F32))
            meanq = E(nc.sbuf_tensor("meanq", [128, T], F32))
            varq = E(nc.sbuf_tensor("varq", [128, T], F32))
            msqq = E(nc.sbuf_tensor("msqq", [128, T], F32))
            lnvq = E(nc.sbuf_tensor("lnvq", [128, T], F32))
            rstdq = E(nc.sbuf_tensor("rstdq", [128, T], F32))
            nmrq = E(nc.sbuf_tensor("nmrq", [128, T], F32))
            ps_t = [E(nc.psum_tensor(f"ps_t{i}", [128, 8, 128], BF16)) for i in range(2)]
            ps_q = [E(nc.psum_tensor(f"ps_q{i}", [128, 512], F32)) for i in range(4)]

            with nc.Block() as blk:
                @blk.gpsimd
                def _(g):
                    g.dma_start(out=identG[:], in_=id_h[:]).then_inc(s_id.h, 16)
                    s_id.n += 16
                    for name, dst, src in (("bq", bq_s, bq_h), ("bk", bk_s, bk_h),
                                           ("bv", bv_s, bv_h)):
                        d = g.dma_start(out=dst[:], in_=src[:])
                        sbl.inc(d, 16)
                    for ch in range(NQ):
                        for i in range(QTS):
                            t = ch * QTS + i
                            if t >= 3:
                                g.wait_ge(sh1.h, t - 2)
                                g.wait_ge(sx[t % 3].h, 16 * (t // 3))
                            d = g.dma_start(out=xb[t % 3][:],
                                            in_=x_h[t * 128:(t + 1) * 128, :])
                            sx[t % 3].inc(d, 16)
                        for f in range(24):
                            fg = ch * 24 + f
                            if fg >= 2:
                                g.wait_ge(smm.h, SC * (fg - 1))
                                g.wait_ge(swl[fg % 2].h, 16 * (fg // 2))
                            src = wq_h[f] if f < 8 else (
                                wk_h[f - 8] if f < 16 else wv_h[f - 16])
                            d = g.dma_start(out=wbuf[fg % 2][:], in_=src)
                            swl[fg % 2].inc(d, 16)

                @blk.sync
                def _(sy):
                    for ch in range(NQ):
                        for f in range(24):
                            for s_i in range(SC):
                                e = (ch * 24 + f) * SC + s_i
                                sy.wait_ge(sqe.h, e + 1)
                                if e >= 4:
                                    sy.wait_ge(sqs[e % 4].h, 16 * (e // 4))
                                off = ch * QTS * 128 + s_i * 512
                                if f < 8:
                                    dst = q_hh[f, :, off:off + 512]
                                elif f < 16:
                                    dst = k_hh[f - 8, :, off:off + 512]
                                else:
                                    dst = v_hh[f - 16, :, off:off + 512]
                                d = sy.dma_start(out=dst, in_=qev[e % 4][:])
                                sqs[e % 4].inc(d, 16)
                    for ss in sqs:
                        sy.wait_ge(ss.h, ss.n)

                @blk.scalar
                def _(s):
                    s.wait_ge(sbl.h, 48)
                    for ch in range(NQ):
                        for i in range(QTS):
                            t = ch * QTS + i
                            if t >= 2:
                                s.wait_ge(stp.h, 4 * (t - 1))
                            s.wait_ge(sx[t % 3].h, 16 * (t // 3 + 1))
                            a = s.activation(h1[t % 2][:], xb[t % 3][:], AF.Square,
                                             accum_out=s2q[:, t:t + 1])
                            ssq.inc(a)
                            s.wait_ge(sv2.h, t + 1)
                            a = s.activation(lnvq[:, t:t + 1], varq[:, t:t + 1], AF.Ln)
                            qch_s.rt(s, a)
                            a = s.activation(rstdq[:, t:t + 1], lnvq[:, t:t + 1],
                                             AF.Exp, scale=-0.5)
                            srs.inc(a)
                            s.wait_ge(snm.h, t + 1)
                            a = s.activation(h1[t % 2][:], xb[t % 3][:], AF.Identity,
                                             bias=nmrq[:, t:t + 1],
                                             scale=rstdq[:, t:t + 1])
                            sh1.inc(a)
                        for f in range(24):
                            for s_i in range(SC):
                                grp = (ch * 24 + f) * SC + s_i
                                s.wait_ge(smm.h, grp + 1)
                                if grp >= 4:
                                    s.wait_ge(sqs[grp % 4].h, 16 * (grp // 4))
                                if f < 8:
                                    bias = bq_s[:, f:f + 1]
                                elif f < 16:
                                    bias = bk_s[:, f - 8:f - 7]
                                else:
                                    bias = bv_s[:, f - 16:f - 15]
                                a = s.activation(qev[grp % 4][:], ps_q[grp % 4][:],
                                                 AF.Identity, bias=bias)
                                sqe.inc(a)

                @blk.vector
                def _(v):
                    for t in range(T):
                        v.wait_ge(sx[t % 3].h, 16 * (t // 3 + 1))
                        a = v.reduce_sum(s1q[:, t:t + 1], xb[t % 3][:], axis=AX.X)
                        qch_v.rt(v, a)
                        a = v.tensor_scalar(out=meanq[:, t:t + 1], in0=s1q[:, t:t + 1],
                                            scalar1=1.0 / C, scalar2=None, op0=ALU.mult)
                        qch_v.rt(v, a)
                        v.wait_ge(ssq.h, t + 1)
                        v.tensor_mul(msqq[:, t:t + 1], meanq[:, t:t + 1],
                                     meanq[:, t:t + 1])
                        a = v.tensor_scalar(out=varq[:, t:t + 1], in0=s2q[:, t:t + 1],
                                            scalar1=1.0 / C, scalar2=EPS,
                                            op0=ALU.mult, op1=ALU.add)
                        qch_v.rt(v, a)
                        a = v.tensor_sub(varq[:, t:t + 1], varq[:, t:t + 1],
                                         msqq[:, t:t + 1])
                        sv2.inc(a)
                        v.wait_ge(srs.h, t + 1)
                        a = v.scalar_tensor_tensor(out=nmrq[:, t:t + 1],
                                                   in0=meanq[:, t:t + 1],
                                                   scalar=-1.0, in1=rstdq[:, t:t + 1],
                                                   op0=ALU.mult, op1=ALU.mult)
                        snm.inc(a)
                        i = t % QTS
                        for b in range(4):
                            bg = t * 4 + b
                            v.wait_ge(stp.h, bg + 1)
                            a = v.tensor_copy(
                                h1T[:, 8 * b:8 * b + 8, i * 128:(i + 1) * 128],
                                ps_t[bg % 2][:])
                            sev.inc(a)

                @blk.tensor
                def _(tn):
                    tn.wait_ge(s_id.h, 16)

                    def emit_tile(t):
                        tn.wait_ge(sh1.h, t + 1)
                        for b in range(4):
                            bg = t * 4 + b
                            if bg >= 2:
                                tn.wait_ge(sev.h, bg - 1)
                            last = None
                            for j in range(8):
                                ct = 8 * b + j
                                last = tn.transpose(
                                    ps_t[bg % 2][:, j, :],
                                    h1[t % 2][:, ct * 128:(ct + 1) * 128],
                                    identG[:])
                            stp.inc(last)

                    for ch in range(NQ):
                        for i in range(QTS):
                            emit_tile(ch * QTS + i)
                        for f in range(24):
                            fg = ch * 24 + f
                            tn.wait_ge(swl[fg % 2].h, 16 * (fg // 2 + 1))
                            for s_i in range(SC):
                                grp = fg * SC + s_i
                                need = ch * QTS + min(4 * (s_i + 1), QTS)
                                tn.wait_ge(sev.h, 4 * need)
                                if grp >= 4:
                                    tn.wait_ge(sqe.h, grp - 3)
                                last = None
                                for ct in range(CT):
                                    last = tn.matmul(
                                        ps_q[grp % 4][:],
                                        wbuf[fg % 2][:, ct, :],
                                        h1T[:, ct, s_i * 512:(s_i + 1) * 512],
                                        start=(ct == 0), stop=(ct == CT - 1))
                                smm.inc(last)

        with ExitStack() as ctx:
            E = ctx.enter_context
            kT = E(nc.sbuf_tensor("kT", [128, 8, S], BF16))
            v_sb = E(nc.sbuf_tensor("v_sb", [128, T, D], BF16))
            wo_sb = E(nc.sbuf_tensor("wo_sb", [128, 8, C], BF16))
            bo_sb = E(nc.sbuf_tensor("bo_sb", [128, C], BF16))
            qt = [E(nc.sbuf_tensor(f"qt{i}", [128, 8, 128], BF16)) for i in range(2)]
            probs = E(nc.sbuf_tensor("probs", [128, S], BF16))
            attnT = E(nc.sbuf_tensor("attnT", [128, T, 128], BF16))
            o_sb = E(nc.sbuf_tensor("o_sb", [128, D], BF16))
            oT = [E(nc.sbuf_tensor(f"oT{i}", [128, 8, 128], BF16)) for i in range(2)]
            pb2 = [E(nc.sbuf_tensor(f"pb2_{i}", [128, C], BF16)) for i in range(2)]
            xr = [E(nc.sbuf_tensor(f"xr{i}", [128, C], BF16)) for i in range(2)]
            x1b = [E(nc.sbuf_tensor(f"x1b{i}", [128, C], BF16)) for i in range(2)]
            negmax = E(nc.sbuf_tensor("negmax", [128, T], F32))
            nmax32 = E(nc.sbuf_tensor("nmax32", [128, T], F32))
            sumexp = E(nc.sbuf_tensor("sumexp", [128, T], F32))
            rs_t = E(nc.sbuf_tensor("rs_t", [128, T], F32))
            s1p = E(nc.sbuf_tensor("s1p", [128, 8], F32))
            s1a = E(nc.sbuf_tensor("s1a", [128, T], F32))
            s2a = E(nc.sbuf_tensor("s2a", [128, T], F32))
            meana = E(nc.sbuf_tensor("meana", [128, T], F32))
            vara = E(nc.sbuf_tensor("vara", [128, T], F32))
            msqa = E(nc.sbuf_tensor("msqa", [128, T], F32))
            lnva = E(nc.sbuf_tensor("lnva", [128, T], F32))
            rstda = E(nc.sbuf_tensor("rstda", [128, T], F32))
            nmra = E(nc.sbuf_tensor("nmra", [128, T], F32))
            ps_s = E(nc.psum_tensor("ps_s", [128, S], F32))
            ps_at = E(nc.psum_tensor("ps_at", [128, 8, 128], BF16))
            ps_p = [E(nc.psum_tensor(f"ps_p{i}", [128, 512], F32))
                    for i in range(3)]
            if S >= 2 * D:
                ps_o = ps_s[:, S - D:S]
            else:
                ps_o = E(nc.psum_tensor("ps_o", [128, D], F32))

            with nc.Block() as blk:
                @blk.sync
                def _(sy):
                    for dt in range(4):
                        d = sy.dma_start(out=kT[:, dt, :], in_=k_hh[dt])
                        skt.inc(d, 16)
                    for t in range(2):
                        src = q_hh[:, :, t * 128:(t + 1) * 128].rearrange(
                            "a p m -> p a m")
                        d = sy.dma_start(out=qt[t % 2][:], in_=src)
                        sq[t % 2].inc(d, 16)
                    for kt_i in range(T):
                        src = v_hh[:, :, kt_i * 128:(kt_i + 1) * 128].rearrange(
                            "a p m -> (a p) m")
                        d = sy.dma_start_transpose(v_sb[:, kt_i, :], src)
                        svtg[kt_i // 4].inc(d, 16)
                    for t in range(2, T):
                        sy.wait_ge(s_sc.h, t - 1)
                        sy.wait_ge(sq[t % 2].h, 16 * (t // 2))
                        src = q_hh[:, :, t * 128:(t + 1) * 128].rearrange(
                            "a p m -> p a m")
                        d = sy.dma_start(out=qt[t % 2][:], in_=src)
                        sq[t % 2].inc(d, 16)
                        j = t - 2
                        sy.wait_ge(s_x1.h, j + 1)
                        if j >= 2:
                            sy.wait_ge(s_x1s[j % 2].h, 16 * (j // 2))
                        d = sy.dma_start(out=x1_hh[j], in_=x1b[j % 2][:])
                        s_x1s[j % 2].inc(d, 16)
                    for j in (T - 2, T - 1):
                        sy.wait_ge(s_x1.h, j + 1)
                        sy.wait_ge(s_x1s[j % 2].h, 16 * (j // 2))
                        d = sy.dma_start(out=x1_hh[j], in_=x1b[j % 2][:])
                        s_x1s[j % 2].inc(d, 16)
                    for ss in s_x1s:
                        sy.wait_ge(ss.h, ss.n)

                @blk.gpsimd
                def _(g):
                    for dt in range(8):
                        d = g.dma_start(out=wo_sb[:, dt, :], in_=wo_h[dt])
                        swo.inc(d, 16)
                    d = g.dma_start(out=bo_sb[:], in_=bo_h[:])
                    swo.inc(d, 16)
                    for t in range(T):
                        for j in (2 * t, 2 * t + 1):
                            if j < 2 * G:
                                gi = j // 2
                                if j % 2 == 0:
                                    d = g.dma_start(out=up_bf[gi], in_=up_h[gi])
                                else:
                                    d = g.dma_start(out=dn_bf[gi], in_=dn_h[gi])
                                s_wc.inc(d, 16)
                        if t >= 2:
                            g.wait_ge(s_x1.h, t - 1)
                            g.wait_ge(sxr[t % 2].h, 16 * (t // 2))
                        d = g.dma_start(out=xr[t % 2][:],
                                        in_=x_h[t * 128:(t + 1) * 128, :])
                        sxr[t % 2].inc(d, 16)

                @blk.tensor
                def _(tn):
                    for t in range(T + 1):
                        if t >= 1:
                            tn.wait_ge(s_oe.h, t)
                            tn.wait_ge(s_ae.h, AB * t)
                            if t >= 2:
                                tn.wait_ge(s_ote.h, t - 1)
                            last = None
                            for dt in range(8):
                                last = tn.transpose(
                                    ps_at[:, dt, :],
                                    o_sb[:, dt * 128:(dt + 1) * 128], identG[:])
                            s_ot.inc(last)
                        if t < T:
                            tn.wait_ge(sq[t % 2].h, 16 * (t // 2 + 1))
                            if t == 0:
                                tn.wait_ge(skt.h, 128)
                            if t >= 1:
                                tn.wait_ge(s_ex.h, t)
                                if S >= 2 * D:
                                    tn.wait_ge(s_oe.h, t)
                            last = None
                            for s_i in range(S // 512):
                                for dt in range(8):
                                    last = tn.matmul(
                                        ps_s[:, s_i * 512:(s_i + 1) * 512],
                                        qt[t % 2][:, dt, :],
                                        kT[:, dt, s_i * 512:(s_i + 1) * 512],
                                        start=(dt == 0), stop=(dt == 7))
                            s_sc.inc(last)
                        if t >= 1:
                            tn.wait_ge(s_ote.h, t)
                            if t == 1:
                                tn.wait_ge(swo.h, swo.n)
                            for cq in range(8):
                                pg = 8 * (t - 1) + cq
                                if pg >= 3:
                                    tn.wait_ge(s_pje.h, pg - 2)
                                last = None
                                for dt in range(8):
                                    last = tn.matmul(
                                        ps_p[pg % 3][:],
                                        oT[(t - 1) % 2][:, dt, :],
                                        wo_sb[:, dt, cq * 512:(cq + 1) * 512],
                                        start=(dt == 0), stop=(dt == 7))
                                s_pj.inc(last)
                        if t < T:
                            tn.wait_ge(s_ex.h, t + 1)
                            for b in range(AB):
                                bg = AB * t + b
                                if bg >= 1:
                                    tn.wait_ge(s_ae.h, bg)
                                nb = min(8, T - 8 * b)
                                last = None
                                for j in range(nb):
                                    kt_i = 8 * b + j
                                    last = tn.transpose(
                                        ps_at[:, j, :],
                                        probs[:, kt_i * 128:(kt_i + 1) * 128],
                                        identG[:])
                                s_at.inc(last)
                            if t >= 1:
                                tn.wait_ge(s_oe.h, t)
                            last1 = None
                            for kt_i in range(T):
                                if t == 0 and kt_i % 4 == 0:
                                    tn.wait_ge(svtg[kt_i // 4].h, 64)
                                if kt_i % 8 == 0:
                                    tn.wait_ge(s_ae.h, AB * t + kt_i // 8 + 1)
                                src = attnT[:, kt_i, :]
                                tn.matmul(ps_o[:, 0:512], src,
                                          v_sb[:, kt_i, 0:512],
                                          start=(kt_i == 0), stop=(kt_i == T - 1),
                                          skip_group_check=True)
                                last1 = tn.matmul(ps_o[:, 512:1024], src,
                                                  v_sb[:, kt_i, 512:1024],
                                                  start=(kt_i == 0),
                                                  stop=(kt_i == T - 1),
                                                  skip_group_check=True)
                            s_oa.inc(last1)

                @blk.vector
                def _(v):
                    for t in range(T + 1):
                        if t >= 1:
                            v.wait_ge(s_ot.h, t)
                            if t >= 3:
                                v.wait_ge(s_pj.h, 8 * (t - 2))
                            a = v.tensor_copy(oT[(t - 1) % 2][:], ps_at[:])
                            s_ote.inc(a)
                        if t < T:
                            v.wait_ge(s_sc.h, t + 1)
                            a = v.reduce_max(negmax[:, t:t + 1], ps_s[:], axis=AX.X,
                                             negate=True)
                            ach_v.rt(v, a)
                            a = v.tensor_scalar(out=nmax32[:, t:t + 1],
                                                in0=negmax[:, t:t + 1],
                                                scalar1=ISQ, scalar2=None,
                                                op0=ALU.mult)
                            s_mx.inc(a)
                        if t >= 1:
                            for cq in range(8):
                                pg = 8 * (t - 1) + cq
                                v.wait_ge(s_pj.h, pg + 1)
                                if cq == 0 and t >= 3:
                                    v.wait_ge(s_nm.h, t - 2)
                                a = v.scalar_tensor_tensor(
                                    out=pb2[(t - 1) % 2][:, cq * 512:(cq + 1) * 512],
                                    in0=ps_p[pg % 3][:], scalar=rs_t[:, t - 1:t],
                                    in1=bo_sb[:, cq * 512:(cq + 1) * 512],
                                    op0=ALU.mult, op1=ALU.add,
                                    accum_out=s1p[:, cq:cq + 1])
                                s_pje.inc(a)
                        if t < T:
                            for b in range(AB):
                                v.wait_ge(s_at.h, AB * t + b + 1)
                                if t >= 1 and b == 0:
                                    v.wait_ge(s_oa.h, t)
                                nb = min(8, T - 8 * b)
                                a = v.tensor_copy(attnT[:, 8 * b:8 * b + nb, :],
                                                  ps_at[:, 0:nb, :])
                                s_ae.inc(a)
                            v.wait_ge(s_ex.h, t + 1)
                            a = v.reciprocal(rs_t[:, t:t + 1], sumexp[:, t:t + 1])
                            ach_v.rt(v, a)
                        if t >= 1:
                            v.wait_ge(s_pje.h, 8 * t)
                            a = v.reduce_sum(s1a[:, t - 1:t], s1p[:], axis=AX.X)
                            ach_v.rt(v, a)
                            a = v.tensor_scalar(out=meana[:, t - 1:t],
                                                in0=s1a[:, t - 1:t],
                                                scalar1=1.0 / C, scalar2=None,
                                                op0=ALU.mult)
                            ach_v.rt(v, a)
                            v.wait_ge(s_q2.h, t)
                            v.tensor_mul(msqa[:, t - 1:t], meana[:, t - 1:t],
                                         meana[:, t - 1:t])
                            a = v.tensor_scalar(out=vara[:, t - 1:t],
                                                in0=s2a[:, t - 1:t],
                                                scalar1=1.0 / C, scalar2=EPS,
                                                op0=ALU.mult, op1=ALU.add)
                            ach_v.rt(v, a)
                            a = v.tensor_sub(vara[:, t - 1:t], vara[:, t - 1:t],
                                             msqa[:, t - 1:t])
                            s_v2.inc(a)
                        if t >= 1:
                            v.wait_ge(s_r2.h, t)
                            a = v.scalar_tensor_tensor(
                                out=nmra[:, t - 1:t], in0=meana[:, t - 1:t],
                                scalar=-1.0, in1=rstda[:, t - 1:t],
                                op0=ALU.mult, op1=ALU.mult)
                            s_n2.inc(a)
                            v.wait_ge(s_nm.h, t)
                            v.wait_ge(sxr[(t - 1) % 2].h, 16 * ((t - 1) // 2 + 1))
                            a = v.tensor_add(x1b[(t - 1) % 2][:],
                                             x1b[(t - 1) % 2][:],
                                             xr[(t - 1) % 2][:])
                            s_x1.inc(a)
                            v.wait_ge(s_x1.h, t)
                            a = v.reduce_sum(s1_3[:, t - 1:t],
                                             x1b[(t - 1) % 2][:], axis=AX.X)
                            ach_v.rt(v, a)
                            a = v.tensor_scalar(out=mean3[:, t - 1:t],
                                                in0=s1_3[:, t - 1:t],
                                                scalar1=1.0 / C, scalar2=None,
                                                op0=ALU.mult)
                            ach_v.rt(v, a)
                            v.wait_ge(s_q3.h, t)
                            v.tensor_mul(msq3[:, t - 1:t], mean3[:, t - 1:t],
                                         mean3[:, t - 1:t])
                            a = v.tensor_scalar(out=var3[:, t - 1:t],
                                                in0=s2_3[:, t - 1:t],
                                                scalar1=1.0 / C, scalar2=EPS,
                                                op0=ALU.mult, op1=ALU.add)
                            ach_v.rt(v, a)
                            a = v.tensor_sub(var3[:, t - 1:t], var3[:, t - 1:t],
                                             msq3[:, t - 1:t])
                            s_v3.inc(a)
                            v.wait_ge(s_r3.h, t)
                            a = v.scalar_tensor_tensor(
                                out=nmr3[:, t - 1:t], in0=mean3[:, t - 1:t],
                                scalar=-1.0, in1=rstd3[:, t - 1:t],
                                op0=ALU.mult, op1=ALU.mult)
                            ach_v.rt(v, a)

                @blk.scalar
                def _(s):
                    for dt in range(4, 8):
                        d = s.dma_start(out=kT[:, dt, :], in_=k_hh[dt])
                        skt.inc(d, 16)
                    for t in range(T + 1):
                        if t < T:
                            s.wait_ge(s_mx.h, t + 1)
                            if t >= 1:
                                s.wait_ge(s_at.h, AB * t)
                            a = s.activation(probs[:], ps_s[:], AF.Exp,
                                             bias=nmax32[:, t:t + 1], scale=ISQ,
                                             accum_out=sumexp[:, t:t + 1])
                            s_ex.inc(a)
                        if t < T:
                            s.wait_ge(s_oa.h, t + 1)
                            s.wait_ge(s_ot.h, t)
                            a = s.activation(o_sb[:], ps_o[:], AF.Identity)
                            s_oe.inc(a)
                        if t >= 1:
                            s.wait_ge(s_pje.h, 8 * t)
                            if t >= 3:
                                s.wait_ge(s_x1s[(t - 1) % 2].h,
                                          16 * ((t - 3) // 2 + 1))
                            a = s.activation(x1b[(t - 1) % 2][:],
                                             pb2[(t - 1) % 2][:],
                                             AF.Square, accum_out=s2a[:, t - 1:t])
                            s_q2.inc(a)
                        if t >= 1:
                            s.wait_ge(s_v2.h, t)
                            a = s.activation(lnva[:, t - 1:t], vara[:, t - 1:t], AF.Ln)
                            ach_s.rt(s, a)
                            a = s.activation(rstda[:, t - 1:t], lnva[:, t - 1:t],
                                             AF.Exp, scale=-0.5)
                            s_r2.inc(a)
                            s.wait_ge(s_n2.h, t)
                            a = s.activation(x1b[(t - 1) % 2][:],
                                             pb2[(t - 1) % 2][:], AF.Identity,
                                             bias=nmra[:, t - 1:t],
                                             scale=rstda[:, t - 1:t])
                            s_nm.inc(a)
                            s.wait_ge(s_x1.h, t)
                            a = s.activation(pb2[(t - 1) % 2][:],
                                             x1b[(t - 1) % 2][:], AF.Square,
                                             accum_out=s2_3[:, t - 1:t])
                            s_q3.inc(a)
                            s.wait_ge(s_v3.h, t)
                            a = s.activation(lnv3[:, t - 1:t], var3[:, t - 1:t],
                                             AF.Ln)
                            ach_s.rt(s, a)
                            a = s.activation(rstd3[:, t - 1:t], lnv3[:, t - 1:t],
                                             AF.Exp, scale=-0.5)
                            s_r3.inc(a)

        with ExitStack() as ctx:
            E = ctx.enter_context
            x1c = E(nc.sbuf_tensor("x1c", [128, FTS, C], BF16))
            h2 = [E(nc.sbuf_tensor(f"h2_{i}", [128, C], BF16)) for i in range(2)]
            h2T = E(nc.sbuf_tensor("h2T", [128, CT, FTS * 128], BF16))
            up_sb = [E(nc.sbuf_tensor(f"up_sb{i}", [128, 2, 8, 128], BF16))
                     for i in range(4)]
            dn_sb = [E(nc.sbuf_tensor(f"dn_sb{i}", [128, 8, 256], BF16))
                     for i in range(4)]
            midT = [E(nc.sbuf_tensor(f"midT{i}", [128, 8, FTS * 128], BF16))
                    for i in range(2)]
            f_raw = E(nc.sbuf_tensor("f_raw", [128, FTS, C], BF16))
            scr3 = E(nc.sbuf_tensor("scr3", [128, C], BF16))
            outb = [E(nc.sbuf_tensor(f"outb{i}", [128, C], F32)) for i in range(2)]
            s1p4 = E(nc.sbuf_tensor("s1p4", [128, FTS, G], F32))
            s1_4 = E(nc.sbuf_tensor("s1_4", [128, T], F32))
            s2_4 = E(nc.sbuf_tensor("s2_4", [128, T], F32))
            mean4 = E(nc.sbuf_tensor("mean4", [128, T], F32))
            var4 = E(nc.sbuf_tensor("var4", [128, T], F32))
            msq4 = E(nc.sbuf_tensor("msq4", [128, T], F32))
            lnv4 = E(nc.sbuf_tensor("lnv4", [128, T], F32))
            rstd4 = E(nc.sbuf_tensor("rstd4", [128, T], F32))
            nmr4 = E(nc.sbuf_tensor("nmr4", [128, T], F32))
            ps_m = [E(nc.psum_tensor(f"ps_m{i}", [128, 2, 512], F32))
                    for i in range(2)]
            ps_f = [E(nc.psum_tensor(f"ps_f{i}", [128, 512], F32)) for i in range(2)]
            ps_t3 = [E(nc.psum_tensor(f"ps_t3{i}", [128, 8, 128], BF16))
                     for i in range(2)]

            with nc.Block() as blk:
                @blk.sync
                def _(sy):
                    for c in range(NF):
                        if c >= 1:
                            sy.wait_ge(sadd.h, FTS * c)
                            sy.wait_ge(sxl.h, 16 * c)
                        src = x1_hh[c * FTS:(c + 1) * FTS].rearrange(
                            "a p m -> p a m")
                        d = sy.dma_start(out=x1c[:], in_=src)
                        sxl.inc(d, 16)
                        for tl in range(FTS):
                            tg = c * FTS + tl
                            sy.wait_ge(sadd.h, tg + 1)
                            if tg >= 2:
                                sy.wait_ge(sost[tg % 2].h, 16 * (tg // 2))
                            d = sy.dma_start(out=out_h[tg * 128:(tg + 1) * 128, :],
                                             in_=outb[tl % 2][:])
                            sost[tg % 2].inc(d, 16)
                    for ss in sost:
                        sy.wait_ge(ss.h, ss.n)

                @blk.gpsimd
                def _(g):
                    g.wait_ge(s_wc.h, 32 * 16)
                    for c in range(NF):
                        for gi in range(G):
                            gg = c * G + gi
                            if gg >= 4:
                                g.wait_ge(sdnt.h, FTS * (gg - 3))
                                g.wait_ge(swf[gg % 4].h, 32 * (gg // 4))
                            d = g.dma_start(out=up_sb[gi % 4][:], in_=up_bf[gi])
                            swf[gg % 4].inc(d, 16)
                            d = g.dma_start(out=dn_sb[gi % 4][:], in_=dn_bf[gi])
                            swf[gg % 4].inc(d, 16)

                @blk.scalar
                def _(s):
                    for c in range(NF):
                        s.wait_ge(sxl.h, 16 * (c + 1))
                        for tl in range(FTS):
                            tg = c * FTS + tl
                            if tg >= 2:
                                s.wait_ge(stp3.h, 4 * (tg - 1))
                            a = s.activation(h2[tg % 2][:], x1c[:, tl, :], AF.Identity,
                                             bias=nmr3[:, tg:tg + 1],
                                             scale=rstd3[:, tg:tg + 1])
                            sh2.inc(a)
                        for gi in range(G):
                            gg = c * G + gi
                            for q in range(4):
                                s.wait_ge(sup.h, 4 * gg + q + 1)
                                if gg >= 2 and q == 0:
                                    s.wait_ge(sdnt.h, FTS * (gg - 1))
                                a = s.activation(midT[gi % 2][:, 2 * q:2 * q + 2, :],
                                                 ps_m[q % 2][:], AF.Gelu)
                                sgel.inc(a)
                        s.wait_ge(sfev.h, FTS * G * (c + 1))
                        for tl in range(FTS):
                            tg = c * FTS + tl
                            if tg >= 1:
                                s.wait_ge(ssq4.h, tg)
                            a = s.activation(scr3[:], f_raw[:, tl, :],
                                             AF.Square,
                                             accum_out=s2_4[:, tg:tg + 1])
                            ssq4.inc(a)
                        s.wait_ge(sv4.h, FTS * (c + 1))
                        a = s.activation(lnv4[:, c * FTS:(c + 1) * FTS],
                                         var4[:, c * FTS:(c + 1) * FTS], AF.Ln)
                        fch_s.rt(s, a)
                        a = s.activation(rstd4[:, c * FTS:(c + 1) * FTS],
                                         lnv4[:, c * FTS:(c + 1) * FTS],
                                         AF.Exp, scale=-0.5)
                        sr4.inc(a)
                        s.wait_ge(sn4.h, c + 1)
                        for tl in range(FTS):
                            tg = c * FTS + tl
                            if tg >= 2:
                                s.wait_ge(sost[tg % 2].h, 16 * (tg // 2))
                            a = s.activation(outb[tl % 2][:], f_raw[:, tl, :],
                                             AF.Identity, bias=nmr4[:, tg:tg + 1],
                                             scale=rstd4[:, tg:tg + 1])
                            snm4.inc(a)

                @blk.vector
                def _(v):
                    for c in range(NF):
                        for tl in range(FTS):
                            tg = c * FTS + tl
                            for b in range(4):
                                bg = tg * 4 + b
                                v.wait_ge(stp3.h, bg + 1)
                                a = v.tensor_copy(
                                    h2T[:, 8 * b:8 * b + 8, tl * 128:(tl + 1) * 128],
                                    ps_t3[bg % 2][:])
                                sev3.inc(a)
                        if c >= 1:
                            v.wait_ge(snm4.h, FTS * c)
                        for gi in range(G):
                            gg = c * G + gi
                            for tt in range(FTS):
                                v.wait_ge(sdnt.h, FTS * gg + (2 if tt < 2 else 4))
                                bank = 0 if tt < 2 else 1
                                a = v.tensor_scalar(
                                    out=f_raw[:, tt, gi * 256:(gi + 1) * 256],
                                    in0=ps_f[bank][:, (tt % 2) * 256:(tt % 2) * 256 + 256],
                                    scalar1=1.0, scalar2=None, op0=ALU.mult,
                                    op1=ALU.add, accum_out=s1p4[:, tt, gi:gi + 1])
                                sfev.inc(a)
                        v.wait_ge(sfev.h, FTS * G * (c + 1))
                        for tl in range(FTS):
                            tg = c * FTS + tl
                            a = v.reduce_sum(s1_4[:, tg:tg + 1], s1p4[:, tl, :],
                                             axis=AX.X)
                            fch_v.rt(v, a)
                            a = v.tensor_scalar(out=mean4[:, tg:tg + 1],
                                                in0=s1_4[:, tg:tg + 1],
                                                scalar1=1.0 / C, scalar2=None,
                                                op0=ALU.mult)
                            fch_v.rt(v, a)
                            v.wait_ge(ssq4.h, tg + 1)
                            v.tensor_mul(msq4[:, tg:tg + 1], mean4[:, tg:tg + 1],
                                         mean4[:, tg:tg + 1])
                            a = v.tensor_scalar(out=var4[:, tg:tg + 1],
                                                in0=s2_4[:, tg:tg + 1],
                                                scalar1=1.0 / C, scalar2=EPS,
                                                op0=ALU.mult, op1=ALU.add)
                            fch_v.rt(v, a)
                            a = v.tensor_sub(var4[:, tg:tg + 1], var4[:, tg:tg + 1],
                                             msq4[:, tg:tg + 1])
                            sv4.inc(a)
                        v.wait_ge(sr4.h, c + 1)
                        a = v.scalar_tensor_tensor(
                            out=nmr4[:, c * FTS:(c + 1) * FTS],
                            in0=mean4[:, c * FTS:(c + 1) * FTS], scalar=-1.0,
                            in1=rstd4[:, c * FTS:(c + 1) * FTS],
                            op0=ALU.mult, op1=ALU.mult)
                        sn4.inc(a)
                        for tl in range(FTS):
                            tg = c * FTS + tl
                            v.wait_ge(snm4.h, tg + 1)
                            a = v.tensor_add(outb[tl % 2][:], outb[tl % 2][:],
                                             x1c[:, tl, :])
                            sadd.inc(a)

                @blk.tensor
                def _(tn):
                    for c in range(NF):
                        for tl in range(FTS):
                            tg = c * FTS + tl
                            tn.wait_ge(sh2.h, tg + 1)
                            for b in range(4):
                                bg = tg * 4 + b
                                if bg >= 2:
                                    tn.wait_ge(sev3.h, bg - 1)
                                last = None
                                for j in range(8):
                                    ct = 8 * b + j
                                    last = tn.transpose(
                                        ps_t3[bg % 2][:, j, :],
                                        h2[tg % 2][:, ct * 128:(ct + 1) * 128],
                                        identG[:])
                                stp3.inc(last)
                        tn.wait_ge(sev3.h, 4 * FTS * (c + 1))
                        for gi in range(G + 1):
                            gg = c * G + gi
                            if gi < G:
                                tn.wait_ge(swf[gg % 4].h, 32 * (gg // 4 + 1))
                                for q in range(4):
                                    qg = 4 * gg + q
                                    if qg >= 2:
                                        tn.wait_ge(sgel.h, qg - 1)
                                    last = None
                                    for mi in range(2):
                                        mt = 2 * q + mi
                                        for kt_i in range(2):
                                            last = tn.matmul(
                                                ps_m[q % 2][:, mi, :],
                                                up_sb[gi % 4][:, kt_i, mt, :],
                                                h2T[:, 2 * gi + kt_i, :],
                                                start=(kt_i == 0), stop=(kt_i == 1))
                                    sup.inc(last)
                            if gi >= 1:
                                gp = gi - 1
                                ggp = c * G + gp
                                tn.wait_ge(sgel.h, 4 * ggp + 4)
                                for tt in range(FTS):
                                    bank = 0 if tt < 2 else 1
                                    if ggp >= 1:
                                        tn.wait_ge(sfev.h, FTS * (ggp - 1)
                                                   + (2 if tt < 2 else 4))
                                    last = None
                                    for mt in range(8):
                                        last = tn.matmul(
                                            ps_f[bank][:, (tt % 2) * 256:(tt % 2) * 256 + 256],
                                            midT[gp % 2][:, mt, tt * 128:(tt + 1) * 128],
                                            dn_sb[gp % 4][:, mt, :],
                                            start=(mt == 0), stop=(mt == 7))
                                    sdnt.inc(last)

    nc.finalize()
    return nc


def _pack(L0_attn_x, L0_attn_Wqkv, L0_attn_bqkv, L0_attn_Wo, L0_attn_bo,
          L0_ffn_up, L0_ffn_down, L0_ffn_perm, S=2048):
    perm = np.asarray(L0_ffn_perm)
    x = np.asarray(L0_attn_x, dtype=np.float32)[:, :S, :]
    x_p = np.ascontiguousarray(np.take(x, perm, axis=2))
    Wqkv = np.asarray(L0_attn_Wqkv, dtype=np.float32)
    Wqkv_p = np.ascontiguousarray(Wqkv[perm, :])
    bqkv = np.asarray(L0_attn_bqkv, dtype=np.float32)
    Wo = np.asarray(L0_attn_Wo, dtype=np.float32)
    Wo_p = np.ascontiguousarray(Wo[:, perm])
    bo_p = np.asarray(L0_attn_bo, dtype=np.float32)[perm]
    up = np.asarray(L0_ffn_up, dtype=np.float32)
    down = np.asarray(L0_ffn_down, dtype=np.float32)

    def wtile(w):
        return np.ascontiguousarray(
            w.reshape(CT, 128, 8, 128).transpose(2, 1, 0, 3).reshape(8, 128, C))

    shared = {
        "wq": wtile(Wqkv_p[:, 0:1024]),
        "wk": wtile(Wqkv_p[:, 1024:2048]),
        "wv": wtile(Wqkv_p[:, 2048:3072]),
        "bq": np.ascontiguousarray(bqkv[0:1024].reshape(8, 128).T),
        "bk": np.ascontiguousarray(bqkv[1024:2048].reshape(8, 128).T),
        "bv": np.ascontiguousarray(bqkv[2048:3072].reshape(8, 128).T),
        "wo": np.ascontiguousarray(Wo_p.reshape(8, 128, C)),
        "bo": np.ascontiguousarray(np.broadcast_to(bo_p, (128, C))),
        "upw": np.ascontiguousarray(
            up.reshape(G, 2, 128, 8, 128).transpose(0, 2, 1, 3, 4).reshape(G, 128, 2048)),
        "dnw": np.ascontiguousarray(
            down.reshape(G, 8, 128, 256).transpose(0, 2, 1, 3).reshape(G, 128, 2048)),
        "ident": np.eye(128, dtype=ml_dtypes.bfloat16),
    }
    in_maps = []
    for c in range(x_p.shape[0]):
        m = dict(shared)
        m["x"] = np.ascontiguousarray(x_p[c])
        in_maps.append(m)
    return in_maps, perm


def kernel(L0_attn_x, L0_attn_Wqkv, L0_attn_bqkv, L0_attn_Wo, L0_attn_bo,
           L0_ffn_up, L0_ffn_down, L0_ffn_perm):
    S = L0_attn_x.shape[1]
    in_maps, perm = _pack(L0_attn_x, L0_attn_Wqkv, L0_attn_bqkv, L0_attn_Wo,
                          L0_attn_bo, L0_ffn_up, L0_ffn_down, L0_ffn_perm, S=S)
    nc = build_program(S=S)
    res = run_bass_kernel_spmd(nc, in_maps, core_ids=list(range(len(in_maps))))
    out_p = np.stack([np.asarray(r["out"]) for r in res.results], axis=0)
    iperm = np.argsort(perm)
    return np.ascontiguousarray(np.take(out_p, iperm, axis=2)).astype(np.float32)
